# version 06 (final)
# baseline (speedup 1.0000x reference)
import contextlib
import numpy as np
import ml_dtypes
import concourse.bass as bass
import concourse.mybir as mybir
from concourse.bass_core_rust import IndirectOffsetOnAxis
from concourse.bass_utils import run_bass_kernel_spmd

F32 = mybir.dt.float32
BF16 = mybir.dt.bfloat16
I32 = mybir.dt.int32
AF = mybir.ActivationFunctionType
ALU = mybir.AluOpType
AX = mybir.AxisListType
BF16NP = ml_dtypes.bfloat16

R = 8
B, S, D = 16, 2048, 1024
SEQ_PER_CORE = B // R
T = SEQ_PER_CORE * S
NCH = D // 128
DFF = 4096
NE, TOPK, CAP = 16, 2, 5120
EPC = NE // R
SENT = 1.0e9
DBG = False
DBG_ARENA = False
RT_LITE = 0

ENGINES = ("sync", "gpsimd", "vector", "scalar", "tensor")
DMA_QUEUES = ("sync", "gpsimd", "scalar")
NLANE = 4



class Op:
    __slots__ = ("eng", "fn", "reads", "writes", "kind", "idx", "waits", "tag",
                 "nbytes")

    def __init__(self, eng, fn, reads, writes, kind, tag=""):
        self.eng, self.fn = eng, fn
        self.reads, self.writes = reads, writes
        self.kind, self.tag = kind, tag
        self.idx = -1
        self.waits = []


def _flat(keys):
    out = []
    for k in keys:
        if isinstance(k, (list, tuple, set)) and not isinstance(k, str):
            out.extend(k)
        else:
            out.append(k)
    return out


class Build:
    def __init__(self):
        self.ops = []

    def emit(self, eng, fn, reads=(), writes=(), kind="compute", tag=""):
        op = Op(eng, fn, _flat(reads), _flat(writes), kind, tag)
        op.nbytes = 0
        op.idx = len(self.ops)
        self.ops.append(op)
        return op

    def finalize(self):
        n = len(self.ops)
        deps = [set() for _ in range(n)]
        last_writer = {}
        readers = {}
        for op in self.ops:
            for k in op.reads:
                w = last_writer.get(k)
                if w is not None:
                    deps[op.idx].add(w)
                readers.setdefault(k, []).append(op.idx)
            for k in op.writes:
                w = last_writer.get(k)
                if w is not None:
                    deps[op.idx].add(w)
                for rd in readers.get(k, ()):
                    if rd != op.idx:
                        deps[op.idx].add(rd)
                last_writer[k] = op.idx
                readers[k] = []
        need_inc = set()
        op_edges = [[] for _ in range(n)]
        for ci in range(n):
            c = self.ops[ci]
            for pi in deps[ci]:
                p = self.ops[pi]
                if (p.eng == c.eng and p.kind == "compute"
                        and c.kind not in ("dma", "coll")):
                    continue
                need_inc.add(pi)
                op_edges[ci].append(pi)
        cnt = {}
        lane_rr = {}
        sem_val = {}
        for op in self.ops:
            if op.kind == "dma":
                nb = getattr(op, "nbytes", 0)
                if nb >= 32768:
                    key = (op.eng, "big")
                    lane = lane_rr.get(key, 0)
                    lane_rr[key] = 1 - lane
                elif nb >= 8192:
                    lane = 2
                else:
                    lane = 3
                sk, step = f"dma_{op.eng}{lane}", 16
            elif op.kind == "coll":
                sk, step = "coll", 1
            elif op.idx in need_inc:
                sk, step = "eng_" + op.eng, 1
            else:
                continue
            cnt[sk] = cnt.get(sk, 0) + step
            sem_val[op.idx] = (sk, cnt[sk], step)
        watermark = {e: {} for e in ENGINES}
        for op in self.ops:
            want = {}
            for pi in op_edges[op.idx]:
                sk, v, _ = sem_val[pi]
                if want.get(sk, 0) < v:
                    want[sk] = v
            for sk, v in want.items():
                wm = watermark[op.eng]
                if wm.get(sk, 0) >= v:
                    continue
                wm[sk] = v
                op.waits.append((sk, v))
        self.sem_val = sem_val
        self.need_inc = set(sem_val.keys())

    def emit_program(self, nc, sems):
        self.finalize()
        by_eng = {e: [] for e in ENGINES}
        for op in self.ops:
            by_eng[op.eng].append(op)

        def stream(eng_obj, ops):
            for op in ops:
                for sk, v in op.waits:
                    eng_obj.wait_ge(sems[sk], v)
                inst = op.fn(eng_obj)
                sv = self.sem_val.get(op.idx)
                if sv is not None:
                    sk, v, step = sv
                    inst.then_inc(sems[sk], step)
                    if op.eng in ("vector", "gpsimd", "scalar") and \
                            sk.startswith("eng_"):
                        eng_obj.wait_ge(sems[sk], v)

        with nc.Block() as block:
            @block.sync
            def _(e):
                stream(e, by_eng["sync"])

            @block.gpsimd
            def _(e):
                stream(e, by_eng["gpsimd"])

            @block.vector
            def _(e):
                stream(e, by_eng["vector"])

            @block.scalar
            def _(e):
                stream(e, by_eng["scalar"])

            @block.tensor
            def _(e):
                stream(e, by_eng["tensor"])



PAGE = 2048


class View:
    __slots__ = ("ap", "keys", "off", "nbytes")

    def __init__(self, ap, keys, off, nbytes):
        self.ap, self.keys, self.off, self.nbytes = ap, keys, off, nbytes

    def __getitem__(self, sl):
        return self.ap[sl]


class Arena:
    def __init__(self, nc, ctx, name, nbytes):
        self.name = name
        self.t = ctx.enter_context(nc.sbuf_tensor(name, [128, nbytes], mybir.dt.uint8))
        self.nbytes = nbytes
        self.cur = 0
        self.stack = []

    def alloc(self, cols, dtype, parts=128):
        esz = {F32: 4, BF16: 2, I32: 4, mybir.dt.uint32: 4}[dtype]
        nb = cols * esz
        nb_al = (nb + PAGE - 1) // PAGE * PAGE
        off = self.cur
        assert off + nb_al <= self.nbytes, (
            f"arena {self.name} overflow: {off}+{nb_al}>{self.nbytes}")
        self.cur = off + nb_al
        ap = self.t[:parts, off:off + nb].bitcast(dtype)
        keys = [f"{self.name}:{p}" for p in range(off // PAGE, (off + nb_al) // PAGE)]
        return View(ap, keys, off, nb_al)

    def mark(self):
        self.stack.append(self.cur)

    def release(self):
        self.cur = self.stack.pop()


class PS:

    def __init__(self, nc, ctx):
        self.banks = []
        for i in range(8):
            t = ctx.enter_context(nc.psum_tensor(f"pb{i}", [128, 512], F32))
            self.banks.append(t)

    def __getitem__(self, i):
        return self.banks[i]

    @staticmethod
    def key(i):
        return f"psum:{i}"



def host_constants():
    c = {}
    c["idn_f32"] = np.eye(128, dtype=np.float32)
    c["idn_bf16"] = np.eye(128, dtype=np.float32).astype(BF16NP)
    c["triu"] = np.triu(np.ones((128, 128), np.float32), 1).astype(BF16NP)
    hd = 64
    inv_freq = 1.0 / (10000.0 ** (np.arange(0, hd, 2, dtype=np.float32) / hd))
    t = np.arange(S, dtype=np.float32)[:, None] * inv_freq[None, :]
    emb = np.concatenate([t, t], axis=1)
    cos = emb.T
    c["rope_cos"] = np.concatenate([np.cos(cos)] * 2, axis=0).astype(BF16NP)
    c["rope_sin"] = np.concatenate([np.sin(cos)] * 2, axis=0).astype(BF16NP)
    P_ = np.zeros((128, 128), np.float32)
    for h0 in (0, 64):
        for i in range(32):
            P_[h0 + 32 + i, h0 + i] = -1.0
            P_[h0 + i, h0 + 32 + i] = 1.0
    c["rope_perm"] = P_.astype(BF16NP)
    masks = []
    for off in (0, 128, 256, 384):
        kk = off + np.arange(128)[:, None]
        qq = np.arange(512)[None, :]
        masks.append((kk <= qq).astype(np.float32))
    c["causal_masks"] = np.stack(masks).astype(BF16NP)
    dbm = []
    for off in (-128, 0, 128, 256, 384):
        kk = off + np.arange(128)[:, None]
        qq = np.arange(512)[None, :]
        d = np.clip(np.abs(kk - qq), 0, 10).astype(np.float32) * 0.01
        dbm.append(d - 0.1)
    c["distbias_m"] = np.stack(dbm).astype(np.float32)
    c["iota16"] = np.broadcast_to(np.arange(16, dtype=np.float32), (1, 16)).copy()
    return c


CONST_SPECS = {
    "idn_f32": ([128, 128], F32),
    "idn_bf16": ([128, 128], BF16),
    "triu": ([128, 128], BF16),
    "rope_cos": ([128, S], BF16),
    "rope_sin": ([128, S], BF16),
    "rope_perm": ([128, 128], BF16),
    "causal_masks": ([4, 128, 512], BF16),
    "distbias_m": ([5, 128, 512], F32),
    "iota16": ([1, 16], F32),
}



class P:

    def __init__(self, nc, bd, dram, A, ps):
        self.nc, self.bd, self.dram, self.A, self.ps = nc, bd, dram, A, ps
        self.sb = {}

    def load_consts(self, names):
        for name in names:
            shape, dt = CONST_SPECS[name]
            if len(shape) == 3:
                n, p_, f = shape
                v = self.A.alloc(n * f, dt, parts=p_)
                for i in range(n):
                    self.dma("sync", v.ap[:, i * f:(i + 1) * f],
                             self.dram[name][i, :, :], dk(name), v.keys)
            else:
                p_, f = shape
                v = self.A.alloc(f, dt, parts=128)
                self.dma("sync", v.ap[0:p_, 0:f], self.dram[name][:, :],
                         dk(name), v.keys)
            self.sb[name] = v

    def dma(self, q, out, in_, reads=(), writes=(), tag="", transpose=False):
        if transpose:
            q = "sync"
        def fn(e):
            return e.dma_start(out=out, in_=in_, transpose=transpose)
        op = self.bd.emit(q, fn, reads, writes, "dma", tag)
        try:
            op.nbytes = out.nbytes() if hasattr(out, "nbytes") else 0
        except Exception:
            op.nbytes = 0
        return op

    def op(self, eng, fn, reads=(), writes=(), kind="compute", tag=""):
        return self.bd.emit(eng, fn, reads, writes, kind, tag)


RP, CP = 128, 512


def dk(name, r0=None, r1=None, c0=None, c1=None):
    shape = DRAM_SPECS[name][0] if name in DRAM_SPECS else CONST_SPECS[name][0]
    nr = int(np.prod(shape[:-1]))
    ncol = shape[-1]
    if r0 is None:
        r0, r1 = 0, nr
    if c0 is None:
        c0, c1 = 0, ncol
    return [f"{name}@{rp}.{cp}"
            for rp in range(r0 // RP, (r1 + RP - 1) // RP)
            for cp in range(c0 // CP, (c1 + CP - 1) // CP)]



def ln_tile(p, xt, out, st, sq, eps=1e-5):
    W = xt.ap.shape[1]
    inv_d = 1.0 / W
    sums, sqs, mu, musq, var, rstd, nmu = (
        st.ap[:, 0:1], st.ap[:, 1:2], st.ap[:, 2:3], st.ap[:, 3:4],
        st.ap[:, 4:5], st.ap[:, 5:6], st.ap[:, 6:7])

    p.op("vector", lambda e: e.reduce_sum(sums, xt.ap, axis=AX.X),
         xt.keys, st.keys, "special")
    p.op("scalar", lambda e: e.activation(sq.ap[:, 0:W], xt.ap, AF.Square),
         xt.keys, sq.keys)
    p.op("vector", lambda e: e.reduce_sum(sqs, sq.ap[:, 0:W], axis=AX.X),
         sq.keys, st.keys, "special")
    p.op("vector", lambda e: e.tensor_scalar_mul(mu, sums, inv_d),
         st.keys, st.keys, "special")
    p.op("vector", lambda e: e.tensor_mul(musq, mu, mu),
         st.keys, st.keys, "special")
    p.op("vector",
         lambda e: e.scalar_tensor_tensor(var, sqs, inv_d, musq,
                                          ALU.mult, ALU.subtract),
         st.keys, st.keys, "special")
    p.op("vector", lambda e: e.tensor_scalar_add(var, var, eps),
         st.keys, st.keys, "special")
    p.op("scalar", lambda e: e.activation(var, var, AF.Ln),
         st.keys, st.keys, "special")
    p.op("scalar", lambda e: e.activation(rstd, var, AF.Exp, scale=-0.5),
         st.keys, st.keys, "special")
    p.op("vector",
         lambda e: e.scalar_tensor_tensor(nmu, mu, -1.0, rstd, ALU.mult,
                                          ALU.mult),
         st.keys, st.keys, "special")
    p.op("scalar",
         lambda e: e.activation(out.ap[:, 0:W], xt.ap, AF.Identity,
                                bias=nmu, scale=rstd),
         xt.keys + st.keys, out.keys)
    return mu, rstd


def ph0_ln0(p):
    A = p.A
    A.mark()
    NB = 4
    xts = [A.alloc(1024, F32) for _ in range(NB)]
    xnbs = [A.alloc(1024, BF16) for _ in range(NB)]
    sts = [A.alloc(16, F32) for _ in range(NB)]
    sqs = [A.alloc(1024, F32) for _ in range(NB)]
    x0 = p.dram["x0"]
    xn0 = p.dram["xn0_tm"]
    for i in range(T // 128):
        r0, r1 = i * 128, (i + 1) * 128
        xt, xnb, st, sq = xts[i % NB], xnbs[i % NB], sts[i % NB], sqs[i % NB]
        p.dma("sync", xt.ap, x0[r0:r1, :], dk("x0", r0, r1), xt.keys)
        ln_tile(p, xt, xnb, st, sq)
        p.dma("scalar", xn0[r0:r1, :], xnb.ap, xnb.keys, dk("xn0_tm", r0, r1))
    A.release()



def ph1_attn(p):
    A = p.A
    nc = p.nc
    A.mark()
    HP = 8
    bproj = A.alloc(8, F32)
    onesf = A.alloc(128, F32)
    NWB = 2
    wstr = [A.alloc(NCH * 128, BF16) for _ in range(NWB)]
    wcnt = [0]

    def lhsT_load(src_name, oc):
        lv = wstr[wcnt[0] % NWB]
        wcnt[0] += 1
        wbf = p.dram[src_name]
        width = wbf.shape[1]
        ap_in = wbf.rearrange("(j p) c -> p j c", p=128)[
            :, :, oc * 128:(oc + 1) * 128]
        def fl(e, lv=lv, ap_in=ap_in):
            return e.dma_start(out=lv.ap.rearrange("p (j c) -> p j c", c=128),
                               in_=ap_in)
        p.bd.emit("gpsimd", fl, dk(src_name), lv.keys, "dma")
        return lv

    for oc in range(8):
        p.dma("sync", bproj.ap[:, oc:oc + 1],
              p.dram["b_proj"][oc * 128:(oc + 1) * 128, :],
              dk("b_proj"), bproj.keys)
    p.op("gpsimd", lambda e: e.memset(onesf.ap, 1.0), (), onesf.keys)
    cosc, sinc, perm, masks = (p.sb["rope_cos"], p.sb["rope_sin"],
                               p.sb["rope_perm"], p.sb["causal_masks"])

    kfm = A.alloc(NCH * S, BF16)
    vto_all = A.alloc(16 * 16 * 65, BF16)
    vto = [View(vto_all.ap[:, k * 1040:(k + 1) * 1040], vto_all.keys, 0, 0)
           for k in range(16)]
    vtraw = A.alloc(1024, BF16)
    exp_all = A.alloc(16 * 512, BF16)
    exps = [View(exp_all.ap[:, k * 512:(k + 1) * 512], exp_all.keys, 0, 0)
            for k in range(16)]
    attn = A.alloc(NCH * 512, BF16)
    rhs_all = A.alloc(NCH * 512, BF16)
    rhsw = [View(rhs_all.ap[:, j * 512:(j + 1) * 512], rhs_all.keys, 0, 0)
            for j in range(NCH)]
    qh = A.alloc(512, BF16)
    qkt = A.alloc(512, BF16)
    rott = A.alloc(512, BF16)
    t1b = A.alloc(512, BF16)
    a64 = A.alloc(512, F32)
    recs = A.alloc(512, F32)
    ps = p.ps

    for s in range(SEQ_PER_CORE):
        base = s * S
        for w in range(4):
            t0 = base + w * 512
            for j in range(NCH):
                p.dma("sync",
                      rhsw[j].ap,
                      p.dram["xn0_tm"][t0:t0 + 512, j * 128:(j + 1) * 128],
                      dk("xn0_tm", t0, t0 + 512, j * 128, (j + 1) * 128),
                      rhsw[j].keys, transpose=True)
            for oc in range(24):
                bank = oc % 2
                lv = lhsT_load("wqkv_bf", oc)
                for j in range(NCH):
                    lhsT = lv.ap[:, j * 128:(j + 1) * 128]
                    def fm(e, lhsT=lhsT, rhs=rhsw[j].ap, bank=bank, j=j):
                        return e.matmul(ps[bank][:, :], lhsT, rhs,
                                        start=(j == 0), stop=(j == NCH - 1))
                    p.op("tensor", fm, lv.keys + rhsw[j].keys,
                         [PS.key(bank)], "compute", f"qkv{oc}")
                if oc < 16:
                    p.op("scalar", lambda e, b=bank: e.activation(
                        qkt.ap, ps[b][:, :], AF.Identity),
                        [PS.key(bank)], qkt.keys)
                    def fr(e):
                        return e.matmul(ps[2][:, :], perm.ap, qkt.ap,
                                        start=True, stop=True)
                    p.op("tensor", fr, qkt.keys + perm.keys, [PS.key(2)])
                    p.op("scalar", lambda e: e.activation(
                        rott.ap, ps[2][:, :], AF.Identity),
                        [PS.key(2)], rott.keys)
                    cs = cosc.ap[:, w * 512:(w + 1) * 512]
                    sn = sinc.ap[:, w * 512:(w + 1) * 512]
                    p.op("vector", lambda e, cs=cs: e.tensor_mul(
                        t1b.ap, qkt.ap, cs), qkt.keys + cosc.keys, t1b.keys)
                    p.op("vector", lambda e, sn=sn: e.tensor_mul(
                        rott.ap, rott.ap, sn), rott.keys + sinc.keys, rott.keys)
                    p.op("vector", lambda e: e.tensor_add(
                        t1b.ap, t1b.ap, rott.ap), t1b.keys + rott.keys, t1b.keys)
                    if oc < 8:
                        p.dma("scalar", p.dram["q_fm"][oc * 128:(oc + 1) * 128,
                                                       t0:t0 + 512],
                              t1b.ap, t1b.keys,
                              dk("q_fm", oc * 128, (oc + 1) * 128, t0, t0 + 512))
                    else:
                        kk = oc - 8
                        kdst = kfm.ap[:, kk * S + w * 512:kk * S + (w + 1) * 512]
                        p.op("vector", lambda e, kdst=kdst: e.tensor_copy(
                            kdst, t1b.ap), t1b.keys, kfm.keys)
                else:
                    p.op("scalar", lambda e, b=bank: e.activation(
                        qkt.ap, ps[b][:, :], AF.Identity),
                        [PS.key(bank)], qkt.keys)
                    oc2 = oc - 16
                    p.dma("scalar", p.dram["v_fm"][oc2 * 128:(oc2 + 1) * 128,
                                                   t0:t0 + 512],
                          qkt.ap, qkt.keys,
                          dk("v_fm", oc2 * 128, (oc2 + 1) * 128, t0, t0 + 512))
        for k in range(16):
            vt = vto[k]
            p.dma("sync", vtraw.ap,
                  p.dram["v_fm"][:, base + k * 128:base + (k + 1) * 128],
                  dk("v_fm", 0, 1024, base + k * 128, base + (k + 1) * 128),
                  vtraw.keys, transpose=True)
            p.op("gpsimd", lambda e, vt=vt: e.memset(vt.ap, 1.0),
                 (), vt.keys)
            p.op("vector", lambda e, vt=vt: e.tensor_copy(
                vt.ap.rearrange("p (h c) -> p h c", c=65)[:, :, 0:64],
                vtraw.ap.rearrange("p (h c) -> p h c", c=64)),
                vtraw.keys, vt.keys)
        for w in range(4):
            t0 = base + w * 512
            nk = (w + 1) * 4
            for hp in range(HP):
                h0, h1 = 2 * hp, 2 * hp + 1
                p.dma("sync", qh.ap,
                      p.dram["q_fm"][hp * 128:(hp + 1) * 128, t0:t0 + 512],
                      dk("q_fm", hp * 128, (hp + 1) * 128, t0, t0 + 512),
                      qh.keys)
                for h, pbase, bank in ((h0, 0, 3), (h1, 64, 4)):
                    for k in range(nk):
                        off = k * 128 - w * 512
                        lhsT = kfm.ap[pbase:pbase + 64,
                                      hp * S + k * 128:hp * S + (k + 1) * 128]
                        rhs = qh.ap[pbase:pbase + 64, :]
                        def fs(e, lhsT=lhsT, rhs=rhs, bank=bank, pbase=pbase):
                            return e.matmul(ps[bank][:, :], lhsT, rhs,
                                            start=True, stop=True,
                                            tile_position=(pbase, 0))
                        p.op("tensor", fs, kfm.keys + qh.keys, [PS.key(bank)])
                        ex = exps[k]
                        p.op("scalar", lambda e, b=bank, ex=ex: e.activation(
                            ex.ap, ps[b][:, :], AF.Exp, scale=0.125),
                            [PS.key(bank)], ex.keys)
                        if off >= 0:
                            mi = off // 128
                            mk = masks.ap[:, mi * 512:(mi + 1) * 512]
                            p.op("vector", lambda e, ex=ex, mk=mk: e.tensor_mul(
                                ex.ap, ex.ap, mk),
                                ex.keys + masks.keys, ex.keys)
                    for k in range(nk):
                        lhsT = vto[k].ap[:, h * 65:(h + 1) * 65]
                        def fa(e, lhsT=lhsT, k=k, last=(k == nk - 1)):
                            return e.matmul(ps[5][0:65, :], lhsT, exps[k].ap,
                                            start=(k == 0), stop=last)
                        p.op("tensor", fa, vto[k].keys + exps[k].keys,
                             [PS.key(5)])
                    p.op("vector", lambda e: e.reciprocal(
                        recs.ap[0:1, :], ps[5][64:65, :]),
                        [PS.key(5)], recs.keys, "special")
                    def fb(e):
                        return e.matmul(ps[6][:, :], onesf.ap[0:1, :],
                                        recs.ap[0:1, :], start=True, stop=True)
                    p.op("tensor", fb, onesf.keys + recs.keys, [PS.key(6)])
                    p.op("scalar", lambda e: e.activation(
                        a64.ap[0:64, :], ps[5][0:64, :], AF.Identity),
                        [PS.key(5)], a64.keys)
                    dst = attn.ap[pbase:pbase + 64, hp * 512:(hp + 1) * 512]
                    p.op("vector", lambda e, dst=dst: e.tensor_mul(
                        dst, a64.ap[0:64, :], ps[6][0:64, :]),
                        a64.keys + [PS.key(6)], attn.keys)
                    if DBG and s == 0 and w == 0 and h == 0:
                        p.dma("gpsimd", p.dram["dbg"][0:128, 0:512],
                              exps[0].ap, exps[0].keys, dk("dbg", 0, 128, 0, 512))
                        p.op("scalar", lambda e: e.activation(
                            a64.ap[0:65, :], ps[5][0:65, :], AF.Identity),
                            [PS.key(5)], a64.keys)
                        p.dma("scalar", p.dram["dbg"][128:193, 512:1024],
                              a64.ap[0:65, :],
                              a64.keys, dk("dbg", 128, 256, 512, 1024))
                        p.dma("scalar", p.dram["dbg"][256:257, 1024:1536],
                              recs.ap[0:1, :], recs.keys,
                              dk("dbg", 256, 257, 1024, 1536))
            for oc in range(8):
                bank = 6 + (oc % 2)
                lv = lhsT_load("wproj_bf", oc)
                for j in range(NCH):
                    lhsT = lv.ap[:, j * 128:(j + 1) * 128]
                    rhs = attn.ap[:, j * 512:(j + 1) * 512]
                    def fp(e, lhsT=lhsT, rhs=rhs, bank=bank, j=j, lv=lv):
                        return e.matmul(ps[bank][:, :], lhsT, rhs,
                                        start=(j == 0), stop=(j == NCH - 1))
                    p.op("tensor", fp, lv.keys + attn.keys,
                         [PS.key(bank)], "compute")
                bias = bproj.ap[:, oc:oc + 1]
                p.op("scalar", lambda e, b=bank, bias=bias: e.activation(
                    qkt.ap, ps[b][:, :], AF.Identity, bias=bias),
                    [PS.key(bank)] + bproj.keys, qkt.keys)
                p.dma("scalar",
                      p.dram["s0_fm"][oc * 128:(oc + 1) * 128, t0:t0 + 512],
                      qkt.ap, qkt.keys,
                      dk("s0_fm", oc * 128, (oc + 1) * 128, t0, t0 + 512))
    A.release()



def wqk(name, oi=None, j=None, no=32, nj=8):
    ois = range(no) if oi is None else [oi]
    js = range(nj) if j is None else [j]
    return [f"{name}${a}.{b}" for a in ois for b in js]


def load_bias_cols(p, name, n, dst):
    for i in range(n):
        p.dma("sync", dst.ap[:, i:i + 1],
              p.dram[name][i * 128:(i + 1) * 128, :], dk(name), dst.keys)


def bcast_row(p, src_row_ap, src_keys, dst, width, bank=7):
    ones = p.sb["onesf"]
    for c0 in range(0, width, 512):
        c1 = min(c0 + 512, width)
        def fb(e, c0=c0, c1=c1):
            return e.matmul(p.ps[bank][:, 0:c1 - c0], ones.ap[0:1, :],
                            src_row_ap[0:1, c0:c1], start=True, stop=True)
        p.op("tensor", fb, src_keys + ones.keys, [PS.key(bank)])
        p.op("scalar", lambda e, c0=c0, c1=c1: e.activation(
            dst.ap[:, c0:c1], p.ps[bank][:, 0:c1 - c0], AF.Identity),
            [PS.key(bank)], dst.keys)


def quant_weight(p, wname, dram_out, nch_rows, width, stv, tag):
    A = p.A
    A.mark()
    stages = [A.alloc(width, F32) for _ in range(2)]
    absbs = [A.alloc(width, F32) for _ in range(2)]
    qis = [A.alloc(width, I32) for _ in range(2)]
    qbs = [A.alloc(width, BF16) for _ in range(2)]
    acc = A.alloc(8, F32)
    no = width // 128
    for j in range(nch_rows):
        stage, absb = stages[j % 2], absbs[j % 2]
        p.dma("sync", stage.ap, p.dram[wname][j * 128:(j + 1) * 128, :],
              dk(wname), stage.keys, tag=tag)
        p.op("scalar", lambda e, stage=stage, absb=absb: e.activation(
            absb.ap, stage.ap, AF.Abs), stage.keys, absb.keys)
        def fr(e, absb=absb):
            return e.reduce_sum(acc.ap[:, 1:2], absb.ap, axis=AX.X)
        p.op("vector", fr, absb.keys, acc.keys, "special")
        if j == 0:
            p.op("vector", lambda e: e.tensor_copy(acc.ap[:, 0:1],
                                                   acc.ap[:, 1:2]),
                 acc.keys, acc.keys, "special")
        else:
            p.op("vector", lambda e: e.tensor_add(acc.ap[:, 0:1],
                                                  acc.ap[:, 0:1],
                                                  acc.ap[:, 1:2]),
                 acc.keys, acc.keys, "special")
    ones = p.sb["onesf"]
    def ft(e):
        return e.matmul(p.ps[6][0:1, 0:1], acc.ap[:, 0:1], ones.ap[:, 0:1],
                        start=True, stop=True)
    p.op("tensor", ft, acc.keys + ones.keys, [PS.key(6)])
    sc = stv
    p.op("vector", lambda e: e.tensor_scalar(
        sc.ap[0:1, 0:1], p.ps[6][0:1, 0:1], 1.0 / (nch_rows * 128 * width),
        1e-5, ALU.mult, ALU.max), [PS.key(6)], sc.keys, "special")
    p.op("vector", lambda e: e.reciprocal(sc.ap[0:1, 1:2], sc.ap[0:1, 0:1]),
         sc.keys, sc.keys, "special")
    def fb2(e):
        return e.matmul(p.ps[6][:, 0:2], ones.ap[0:1, :], sc.ap[0:1, 0:2],
                        start=True, stop=True)
    p.op("tensor", fb2, sc.keys + ones.keys, [PS.key(6)])
    p.op("scalar", lambda e: e.activation(sc.ap[:, 4:6], p.ps[6][:, 0:2],
                                          AF.Identity),
         [PS.key(6)], sc.keys, "special")
    for j in range(nch_rows):
        stage, absb = stages[j % 2], absbs[j % 2]
        qi, qb = qis[j % 2], qbs[j % 2]
        p.dma("sync", stage.ap, p.dram[wname][j * 128:(j + 1) * 128, :],
              dk(wname), stage.keys, tag=tag)
        p.op("scalar", lambda e, stage=stage, absb=absb: e.activation(
            absb.ap, stage.ap, AF.Identity, scale=sc.ap[:, 5:6]),
            stage.keys + sc.keys, absb.keys)
        p.op("vector", lambda e, absb=absb, qi=qi: e.tensor_copy(
            qi.ap, absb.ap), absb.keys, qi.keys)
        p.op("vector", lambda e, qi=qi: e.tensor_scalar(
            qi.ap, qi.ap, -1.0, 1.0, ALU.max, ALU.min), qi.keys, qi.keys)
        p.op("vector", lambda e, qi=qi, qb=qb: e.tensor_copy(qb.ap, qi.ap),
             qi.keys, qb.keys)
        def fw(e, j=j, qb=qb):
            return e.dma_start(
                out=dram_out[:, j, :, :].rearrange("a p c -> p a c"),
                in_=qb.ap.rearrange("p (f c) -> p f c", c=128))
        p.bd.emit("scalar", fw, qb.keys, wqk(tag, None, j, no, nch_rows),
                  "dma", tag)
    A.release()


def ph2_wquant(p):
    stv = p.A.alloc(8, F32)
    p.wq_stats = stv
    onesf = p.A.alloc(128, F32)
    p.op("gpsimd", lambda e: e.memset(onesf.ap, 1.0), (), onesf.keys)
    p.sb["onesf"] = onesf
    stv2 = p.A.alloc(8, F32)
    p.wq_stats2 = stv2
    A = p.A
    A.mark()
    cst = [A.alloc(3072, F32) for _ in range(2)]
    cqb = [A.alloc(3072, BF16) for _ in range(2)]
    for nm, dst, w in (("w_qkv", "wqkv_bf", 3072), ("w_proj", "wproj_bf", D)):
        for j in range(NCH):
            st, qb = cst[j % 2], cqb[j % 2]
            p.dma("sync", st.ap[:, 0:w],
                  p.dram[nm][j * 128:(j + 1) * 128, :], dk(nm), st.keys)
            p.op("vector", lambda e, st=st, qb=qb, w=w: e.tensor_copy(
                qb.ap[:, 0:w], st.ap[:, 0:w]), st.keys, qb.keys)
            p.dma("scalar", p.dram[dst][j * 128:(j + 1) * 128, :],
                  qb.ap[:, 0:w], qb.keys, dk(dst))
    A.release()
    quant_weight(p, "w1T", p.dram["wq1_d"], NCH, DFF, stv, "wq1")
    quant_weight(p, "w2T", p.dram["wq2_d"], DFF // 128, D, stv2, "wq2")


def ph2_sweepA(p):
    A = p.A
    A.mark()
    w1r = A.alloc(1024, F32, parts=1)
    b1r = A.alloc(1024, F32, parts=1)
    p.dma("sync", w1r.ap[0:1, :],
          p.dram["ln1_w"].rearrange("a 1 -> 1 a"), dk("ln1_w"), w1r.keys)
    p.dma("sync", b1r.ap[0:1, :],
          p.dram["ln1_b"].rearrange("a 1 -> 1 a"), dk("ln1_b"), b1r.keys)
    Wt = A.alloc(1024, F32)
    Bt = A.alloc(1024, F32)
    bcast_row(p, w1r.ap, w1r.keys, Wt, 1024)
    bcast_row(p, b1r.ap, b1r.keys, Bt, 1024)
    NB = 2
    xts = [A.alloc(1024, F32) for _ in range(NB)]
    hts = [A.alloc(1024, F32) for _ in range(NB)]
    qis = [A.alloc(1024, I32) for _ in range(NB)]
    qbs = [A.alloc(1024, BF16) for _ in range(NB)]
    sts = [A.alloc(16, F32) for _ in range(NB)]
    sqs = [A.alloc(1024, F32) for _ in range(NB)]
    colx = A.alloc(128, F32)
    idn = p.sb["idn_f32"]
    clipw_col = p.wq_stats.ap[:, 4:5]
    for i in range(T // 128):
        r0, r1 = i * 128, (i + 1) * 128
        xt, ht, qi2, qb, st, sq = (xts[i % NB], hts[i % NB], qis[i % NB],
                                   qbs[i % NB], sts[i % NB], sqs[i % NB])
        p.dma("gpsimd", xt.ap, p.dram["xn0_tm"][r0:r1, :],
              dk("xn0_tm", r0, r1), xt.keys)
        ln_tile(p, xt, View(sq.ap, sq.keys, 0, 0), st, sq)
        p.op("vector", lambda e, sq=sq, ht=ht: e.tensor_mul(
            ht.ap, sq.ap, Wt.ap), sq.keys + Wt.keys, ht.keys)
        p.op("vector", lambda e, ht=ht: e.tensor_add(
            ht.ap, ht.ap, Bt.ap), ht.keys + Bt.keys, ht.keys)
        am, cl, sx, cf = (st.ap[:, 8:9], st.ap[:, 9:10], st.ap[:, 10:11],
                          st.ap[:, 11:12])
        p.op("vector", lambda e, ht=ht, am=am: e.reduce_max(
            am, ht.ap, axis=AX.X, apply_absolute_value=True),
            ht.keys, st.keys, "special")
        p.op("vector", lambda e, am=am, cl=cl: e.tensor_scalar_max(
            cl, am, 1e-5), st.keys, st.keys, "special")
        p.op("vector", lambda e, cl=cl, sx=sx: e.reciprocal(sx, cl),
             st.keys, st.keys, "special")
        p.op("vector", lambda e, sx=sx: e.tensor_scalar_mul(sx, sx, 127.0),
             st.keys, st.keys, "special")
        p.op("vector", lambda e, cl=cl, cf=cf: e.scalar_tensor_tensor(
            cf, cl, 1.0 / 127.0, clipw_col, ALU.mult, ALU.mult),
            st.keys + p.wq_stats.keys, st.keys, "special")
        p.op("scalar", lambda e, ht=ht, sx=sx: e.activation(
            ht.ap, ht.ap, AF.Identity, scale=sx), st.keys + ht.keys, ht.keys)
        p.op("vector", lambda e, ht=ht, qi2=qi2: e.tensor_copy(qi2.ap, ht.ap),
             ht.keys, qi2.keys)
        p.op("vector", lambda e, qi2=qi2, qb=qb: e.tensor_copy(qb.ap, qi2.ap),
             qi2.keys, qb.keys)
        p.dma("scalar", p.dram["qx_d"][r0:r1, :], qb.ap, qb.keys,
              dk("qx_d", r0, r1))
        def ftr(e, cf=cf):
            return e.matmul(p.ps[6][0:1, 0:128], cf, idn.ap,
                            start=True, stop=True)
        p.op("tensor", ftr, st.keys + idn.keys, [PS.key(6)])
        p.op("vector", lambda e: e.tensor_copy(colx.ap[0:1, :],
                                               p.ps[6][0:1, 0:128]),
             [PS.key(6)], colx.keys, "special")
        p.dma("scalar", p.dram["colf1_d"][0:1, r0:r1], colx.ap[0:1, :],
              colx.keys, dk("colf1_d", 0, 1, r0, r1))
    A.release()


def _mm_ffn(p, qx_name, wq_name, wq_dram, colf_name, nin_ch, nout_ch,
            epilogue, tag):
    A = p.A
    A.mark()
    NBUF = 2
    rhs = [A.alloc(512, BF16) for _ in range(NBUF * nin_ch)]
    lw = [A.alloc(nin_ch * 128, BF16) for _ in range(NBUF)]
    colr = A.alloc(512, F32, parts=1)
    colfb = A.alloc(512, F32)
    for tt in range(T // 512):
        t0 = tt * 512
        bsel = tt % NBUF
        for j in range(nin_ch):
            v = rhs[bsel * nin_ch + j]
            p.dma("sync" if j % 2 == 0 else "scalar", v.ap,
                  p.dram[qx_name][t0:t0 + 512, j * 128:(j + 1) * 128],
                  dk(qx_name, t0, t0 + 512, j * 128, (j + 1) * 128),
                  v.keys, transpose=True)
        p.dma("sync", colr.ap[0:1, :], p.dram[colf_name][0:1, t0:t0 + 512],
              dk(colf_name, 0, 1, t0, t0 + 512), colr.keys)
        bcast_row(p, colr.ap, colr.keys, colfb, 512, bank=6)
        for fc in range(nout_ch):
            lv = lw[(tt * nout_ch + fc) % NBUF]
            def fl(e, lv=lv, fc=fc):
                return e.dma_start(
                    out=lv.ap.rearrange("p (a c) -> p a c", c=128),
                    in_=wq_dram[fc, :, :, :].rearrange("a p c -> p a c"))
            p.bd.emit("gpsimd", fl, wqk(tag, fc, None, nout_ch, nin_ch),
                      lv.keys, "dma", tag)
            bank = fc % 2
            for j in range(nin_ch):
                lhsT = lv.ap[:, j * 128:(j + 1) * 128]
                rv = rhs[bsel * nin_ch + j]
                def fm2(e, lhsT=lhsT, rv=rv, bank=bank, j=j,
                        last=(j == nin_ch - 1)):
                    return e.matmul(p.ps[bank][:, :], lhsT, rv.ap,
                                    start=(j == 0), stop=last)
                p.op("tensor", fm2, lv.keys + rv.keys, [PS.key(bank)])
            epilogue(p, fc, tt, bank, colfb)
    A.release()


def ph2_sweepB(p):
    p.A.mark()
    b1c = p.A.alloc(32, F32)
    load_bias_cols(p, "b1", 32, b1c)
    hsc = p.A.alloc(512, F32)
    hbf = p.A.alloc(512, BF16)

    def epi(p_, fc, tt, bank, colfb):
        t0 = tt * 512
        p_.op("vector", lambda e: e.tensor_mul(
            hsc.ap, p_.ps[bank][:, :], colfb.ap),
            [PS.key(bank)] + colfb.keys, hsc.keys)
        p_.op("scalar", lambda e, fc=fc: e.activation(
            hbf.ap, hsc.ap, AF.Gelu, bias=b1c.ap[:, fc:fc + 1]),
            hsc.keys + b1c.keys, hbf.keys)
        p_.dma("scalar", p_.dram["h_fm"][fc * 128:(fc + 1) * 128, t0:t0 + 512],
               hbf.ap, hbf.keys, dk("h_fm", fc * 128, (fc + 1) * 128,
                                    t0, t0 + 512))
        if DBG and fc == 0 and tt == 1:
            p_.dma("scalar", p_.dram["dbg"][4:132, 0:512], hsc.ap,
                   hsc.keys, dk("dbg", 4, 132, 0, 512))
            p_.dma("gpsimd", p_.dram["dbg"][132:260, 512:1024], hbf.ap,
                   hbf.keys, dk("dbg", 132, 260, 512, 1024))
            p_.op("scalar", lambda e: e.activation(
                hsc.ap, p_.ps[bank][:, :], AF.Identity),
                [PS.key(bank)], hsc.keys)
            p_.dma("scalar", p_.dram["dbg"][260:388, 1024:1536], hsc.ap,
                   hsc.keys, dk("dbg", 260, 388, 1024, 1536))
            p_.dma("scalar", p_.dram["dbg"][388:516, 1536:2048], colfb.ap,
                   colfb.keys, dk("dbg", 388, 516, 1536, 2048))

    _mm_ffn(p, "qx_d", "wq1", p.dram["wq1_d"], "colf1_d", NCH, DFF // 128,
            epi, "wq1")
    p.A.release()


def ph2_sweepC(p):
    A = p.A
    A.mark()
    w2r = A.alloc(DFF, F32, parts=1)
    b2r = A.alloc(DFF, F32, parts=1)
    p.dma("sync", w2r.ap[0:1, :],
          p.dram["ln2_w"].rearrange("a 1 -> 1 a"), dk("ln2_w"), w2r.keys)
    p.dma("sync", b2r.ap[0:1, :],
          p.dram["ln2_b"].rearrange("a 1 -> 1 a"), dk("ln2_b"), b2r.keys)
    Wt = A.alloc(DFF, F32)
    Bt = A.alloc(DFF, F32)
    bcast_row(p, w2r.ap, w2r.keys, Wt, DFF)
    bcast_row(p, b2r.ap, b2r.keys, Bt, DFF)
    NB = 1
    hts = [A.alloc(DFF, BF16) for _ in range(NB)]
    hfs = [A.alloc(DFF, F32) for _ in range(NB)]
    qis = [A.alloc(DFF, I32) for _ in range(NB)]
    qbs = [A.alloc(DFF, BF16) for _ in range(NB)]
    sts = [A.alloc(16, F32) for _ in range(NB)]
    sqs = [A.alloc(DFF, F32) for _ in range(NB)]
    colx = A.alloc(128, F32)
    idn = p.sb["idn_f32"]
    clipw_col = p.wq_stats2.ap[:, 4:5]
    for i in range(T // 128):
        r0, r1 = i * 128, (i + 1) * 128
        ht, hf, qi2, qb, st, sq = (hts[i % NB], hfs[i % NB], qis[i % NB],
                                   qbs[i % NB], sts[i % NB], sqs[i % NB])
        p.dma("sync", ht.ap, p.dram["h_fm"][:, r0:r1],
              dk("h_fm", 0, DFF, r0, r1), ht.keys, transpose=True)
        if DBG and i < 4:
            p.op("vector", lambda e, ht=ht, st=st: e.reduce_max(
                st.ap[:, 12:13], ht.ap, axis=AX.X, apply_absolute_value=True),
                ht.keys, st.keys, "special")
        ln_tile(p, ht, View(sq.ap, sq.keys, 0, 0), st, sq)
        p.op("vector", lambda e, sq=sq, hf=hf: e.tensor_mul(
            hf.ap, sq.ap, Wt.ap), sq.keys + Wt.keys, hf.keys)
        p.op("vector", lambda e, hf=hf: e.tensor_add(
            hf.ap, hf.ap, Bt.ap), hf.keys + Bt.keys, hf.keys)
        am, cl, sx, cf = (st.ap[:, 8:9], st.ap[:, 9:10], st.ap[:, 10:11],
                          st.ap[:, 11:12])
        p.op("vector", lambda e, hf=hf, am=am: e.reduce_max(
            am, hf.ap, axis=AX.X, apply_absolute_value=True),
            hf.keys, st.keys, "special")
        p.op("vector", lambda e, am=am, cl=cl: e.tensor_scalar_max(
            cl, am, 1e-5), st.keys, st.keys, "special")
        p.op("vector", lambda e, cl=cl, sx=sx: e.reciprocal(sx, cl),
             st.keys, st.keys, "special")
        p.op("vector", lambda e, sx=sx: e.tensor_scalar_mul(sx, sx, 127.0),
             st.keys, st.keys, "special")
        p.op("vector", lambda e, cl=cl, cf=cf: e.scalar_tensor_tensor(
            cf, cl, 1.0 / 127.0, clipw_col, ALU.mult, ALU.mult),
            st.keys + p.wq_stats2.keys, st.keys, "special")
        p.op("scalar", lambda e, hf=hf, sx=sx: e.activation(
            hf.ap, hf.ap, AF.Identity, scale=sx), st.keys + hf.keys, hf.keys)
        p.op("vector", lambda e, hf=hf, qi2=qi2: e.tensor_copy(
            qi2.ap, hf.ap), hf.keys, qi2.keys)
        p.op("vector", lambda e, qi2=qi2, qb=qb: e.tensor_copy(qb.ap, qi2.ap),
             qi2.keys, qb.keys)
        p.dma("scalar", p.dram["qx2_d"][r0:r1, :], qb.ap, qb.keys,
              dk("qx2_d", r0, r1))
        def ftr(e, cf=cf):
            return e.matmul(p.ps[6][0:1, 0:128], cf, idn.ap,
                            start=True, stop=True)
        p.op("tensor", ftr, st.keys + idn.keys, [PS.key(6)])
        p.op("vector", lambda e: e.tensor_copy(colx.ap[0:1, :],
                                               p.ps[6][0:1, 0:128]),
             [PS.key(6)], colx.keys, "special")
        p.dma("scalar", p.dram["colf2_d"][0:1, r0:r1], colx.ap[0:1, :],
              colx.keys, dk("colf2_d", 0, 1, r0, r1))
        if DBG and i < 4:
            p.dma("scalar", p.dram["dbg"][2:3, 16 * i:16 * i + 8],
                  p.wq_stats2.ap[0:1, 0:8], p.wq_stats2.keys,
                  dk("dbg", 2, 3, 0, 512))
            p.dma("scalar", p.dram["dbg"][3:4, 16 * i:16 * i + 16],
                  st.ap[0:1, 0:16], st.keys, dk("dbg", 3, 4, 0, 512))
            p.dma("scalar", p.dram["dbg"][0:1, 128 * i:128 * i + 128],
                  colx.ap[0:1, :], colx.keys, dk("dbg", 0, 1, 0, 512))
            p.op("scalar", lambda e, i=i: e.activation(
                colx.ap[32:33, :], p.ps[6][0:1, 0:128], AF.Identity),
                [PS.key(6)], colx.keys, "special")
            p.dma("scalar", p.dram["dbg"][1:2, 128 * i:128 * i + 128],
                  colx.ap[32:33, :], colx.keys, dk("dbg", 1, 2, 0, 512))
    A.release()


def ph2_sweepD(p):
    A = p.A
    A.mark()
    b2c = A.alloc(8, F32)
    load_bias_cols(p, "b2", 8, b2c)
    NBUF = 2
    rhs = [A.alloc(512, BF16) for _ in range(NBUF)]
    lw = [A.alloc(8 * 128, BF16) for _ in range(NBUF)]
    colr = A.alloc(512, F32, parts=1)
    colfb = A.alloc(512, F32)
    gsc = A.alloc(512, F32)
    gbf = A.alloc(512, BF16)
    for tt in range(T // 512):
        t0 = tt * 512
        p.dma("sync", colr.ap[0:1, :], p.dram["colf2_d"][0:1, t0:t0 + 512],
              dk("colf2_d", 0, 1, t0, t0 + 512), colr.keys)
        bcast_row(p, colr.ap, colr.keys, colfb, 512, bank=7)
        for j in range(32):
            rv = rhs[j % NBUF]
            p.dma("sync" if j % 2 else "scalar", rv.ap,
                  p.dram["qx2_d"][t0:t0 + 512, j * 128:(j + 1) * 128],
                  dk("qx2_d", t0, t0 + 512, j * 128, (j + 1) * 128),
                  rv.keys, transpose=True)
            lv = lw[j % NBUF]
            def fl(e, lv=lv, j=j):
                return e.dma_start(
                    out=lv.ap.rearrange("p (a c) -> p a c", c=128),
                    in_=p.dram["wq2_d"][:, j, :, :].rearrange("a p c -> p a c"))
            p.bd.emit("gpsimd", fl, wqk("wq2", None, j, 8, 32),
                      lv.keys, "dma", "wq2")
            for oc in range(8):
                lhsT = lv.ap[:, oc * 128:(oc + 1) * 128]
                def fm2(e, lhsT=lhsT, rv=rv, oc=oc, j=j):
                    return e.matmul(p.ps[oc][:, :], lhsT, rv.ap,
                                    start=(j == 0), stop=(j == 31))
                p.op("tensor", fm2, lv.keys + rv.keys, [PS.key(oc)])
        for oc in range(8):
            p.op("vector", lambda e, oc=oc: e.tensor_mul(
                gsc.ap, p.ps[oc][:, :], colfb.ap),
                [PS.key(oc)] + colfb.keys, gsc.keys)
            p.op("scalar", lambda e, oc=oc: e.activation(
                gbf.ap, gsc.ap, AF.Silu, bias=b2c.ap[:, oc:oc + 1]),
                gsc.keys + b2c.keys, gbf.keys)
            p.dma("scalar",
                  p.dram["g0_fm"][oc * 128:(oc + 1) * 128, t0:t0 + 512],
                  gbf.ap, gbf.keys,
                  dk("g0_fm", oc * 128, (oc + 1) * 128, t0, t0 + 512))
    A.release()


def ph2_sweepE(p):
    A = p.A
    A.mark()
    NB = 2
    xts = [A.alloc(1024, F32) for _ in range(NB)]
    s0s = [A.alloc(1024, BF16) for _ in range(NB)]
    g0s = [A.alloc(1024, BF16) for _ in range(NB)]
    prods = [A.alloc(1024, F32) for _ in range(NB)]
    xnb = [A.alloc(1024, BF16) for _ in range(NB)]
    sts = [A.alloc(16, F32) for _ in range(NB)]
    sqs = [A.alloc(1024, F32) for _ in range(NB)]
    for i in range(T // 128):
        r0, r1 = i * 128, (i + 1) * 128
        xt, s0, g0, pr, xo, st, sq = (xts[i % NB], s0s[i % NB], g0s[i % NB],
                                      prods[i % NB], xnb[i % NB], sts[i % NB],
                                      sqs[i % NB])
        p.dma("sync", xt.ap, p.dram["x0"][r0:r1, :], dk("x0", r0, r1), xt.keys)
        p.dma("sync", s0.ap, p.dram["s0_fm"][:, r0:r1],
              dk("s0_fm", 0, D, r0, r1), s0.keys, transpose=True)
        p.dma("scalar", g0.ap, p.dram["g0_fm"][:, r0:r1],
              dk("g0_fm", 0, D, r0, r1), g0.keys, transpose=True)
        p.op("vector", lambda e, s0=s0, g0=g0, pr=pr: e.tensor_mul(
            pr.ap, s0.ap, g0.ap), s0.keys + g0.keys, pr.keys)
        p.op("vector", lambda e, xt=xt, pr=pr: e.tensor_add(
            xt.ap, xt.ap, pr.ap), xt.keys + pr.keys, xt.keys)
        p.dma("scalar", p.dram["x1_tm"][r0:r1, :], xt.ap, xt.keys,
              dk("x1_tm", r0, r1))
        ln_tile(p, xt, xo, st, sq)
        p.dma("scalar", p.dram["xn1_tm"][r0:r1, :], xo.ap, xo.keys,
              dk("xn1_tm", r0, r1))
    A.release()


def ph2_all(p):
    ph2_sweepA(p)
    ph2_sweepB(p)
    ph2_sweepC(p)
    ph2_sweepD(p)
    ph2_sweepE(p)



def ph4_attn1(p):
    A = p.A
    A.mark()
    wl = A.alloc(NCH * 1024, BF16)
    bl = A.alloc(8, F32)
    for j in range(NCH):
        p.dma("gpsimd", wl.ap[:, j * 1024:(j + 1) * 1024],
              p.dram["l1W"][j * 128:(j + 1) * 128, :], dk("l1W"), wl.keys)
    load_bias_cols(p, "l1b", 8, bl)
    onesb = A.alloc(128, BF16)
    p.op("gpsimd", lambda e: e.memset(onesb.ap, 1.0), (), onesb.keys)
    onesf = p.sb["onesf"]
    dbm = p.sb["distbias_m"]

    ffm = A.alloc(NCH * S, BF16)
    ftm = A.alloc(16 * 1024, BF16)
    exp_all = A.alloc(16 * 512, BF16)
    exps = [View(exp_all.ap[:, k * 512:(k + 1) * 512], exp_all.keys, 0, 0)
            for k in range(16)]
    rhs_all = A.alloc(NCH * 512, BF16)
    rhsw = [View(rhs_all.ap[:, j * 512:(j + 1) * 512], rhs_all.keys, 0, 0)
            for j in range(NCH)]
    fsc = A.alloc(512, F32)
    sqs = A.alloc(512, F32)
    diag = A.alloc(S, F32, parts=1)
    cbuf = A.alloc(512, F32)
    tbuf = A.alloc(512, F32)
    rb = A.alloc(512, F32)
    recs = A.alloc(512, F32)
    sfm = A.alloc(512, BF16)
    ps = p.ps

    for s in range(SEQ_PER_CORE):
        base = s * S
        for w in range(4):
            t0 = base + w * 512
            for j in range(NCH):
                p.dma("sync", rhsw[j].ap,
                      p.dram["xn1_tm"][t0:t0 + 512, j * 128:(j + 1) * 128],
                      dk("xn1_tm", t0, t0 + 512, j * 128, (j + 1) * 128),
                      rhsw[j].keys, transpose=True)
            for oc in range(8):
                bank = oc % 2
                for j in range(NCH):
                    lhsT = wl.ap[:, j * 1024 + oc * 128:j * 1024 + (oc + 1) * 128]
                    def fm(e, lhsT=lhsT, rhs=rhsw[j].ap, bank=bank, j=j):
                        return e.matmul(ps[bank][:, :], lhsT, rhs,
                                        start=(j == 0), stop=(j == NCH - 1))
                    p.op("tensor", fm, wl.keys + rhsw[j].keys,
                         [PS.key(bank)])
                p.op("scalar", lambda e, b=bank, oc=oc: e.activation(
                    fsc.ap, ps[b][:, :], AF.Identity, bias=bl.ap[:, oc:oc + 1]),
                    [PS.key(bank)] + bl.keys, fsc.keys)
                fdst = ffm.ap[:, oc * S + w * 512:oc * S + (w + 1) * 512]
                p.op("vector", lambda e, fdst=fdst: e.tensor_copy(fdst, fsc.ap),
                     fsc.keys, ffm.keys)
                p.dma("scalar",
                      p.dram["f_fm"][oc * 128:(oc + 1) * 128, t0:t0 + 512],
                      fdst, ffm.keys,
                      dk("f_fm", oc * 128, (oc + 1) * 128, t0, t0 + 512))
                p.op("scalar", lambda e: e.activation(sqs.ap, fsc.ap,
                                                      AF.Square),
                     fsc.keys, sqs.keys)
                def fsum(e, oc=oc):
                    return e.matmul(ps[7][0:1, :], onesf.ap[:, 0:1], sqs.ap,
                                    start=(oc == 0), stop=(oc == 7))
                p.op("tensor", fsum, sqs.keys + onesf.keys, [PS.key(7)])
            p.op("vector", lambda e, w=w, s=s: e.tensor_scalar_add(
                diag.ap[0:1, w * 512:(w + 1) * 512], ps[7][0:1, :], 0.1),
                [PS.key(7)], diag.keys, "special")
        for w in range(4):
            t0 = base + w * 512
            def fcb(e, w=w):
                return e.matmul(ps[6][:, :], onesf.ap[0:1, :],
                                diag.ap[0:1, w * 512:(w + 1) * 512],
                                start=True, stop=True)
            p.op("tensor", fcb, onesf.keys + diag.keys, [PS.key(6)])
            p.op("scalar", lambda e: e.activation(cbuf.ap, ps[6][:, :],
                                                  AF.Identity),
                 [PS.key(6)], cbuf.keys)
            if w == 0:
                for k in range(16):
                    dstv = ftm.ap[:, k * 1024:(k + 1) * 1024]
                    p.dma("sync", dstv,
                          p.dram["f_fm"][:, base + k * 128:base + (k + 1) * 128],
                          dk("f_fm", 0, D, base + k * 128, base + (k + 1) * 128),
                          ftm.keys, transpose=True)
            for k in range(16):
                bank = 2 + (k % 2)
                for j in range(NCH):
                    lhsT = ffm.ap[:, j * S + k * 128:j * S + (k + 1) * 128]
                    rhs = ffm.ap[:, j * S + w * 512:j * S + (w + 1) * 512]
                    def fe(e, lhsT=lhsT, rhs=rhs, bank=bank, j=j):
                        return e.matmul(ps[bank][:, :], lhsT, rhs,
                                        start=(j == 0), stop=(j == NCH - 1))
                    p.op("tensor", fe, ffm.keys, [PS.key(bank)])
                p.op("vector", lambda e, b=bank: e.tensor_sub(
                    tbuf.ap, ps[b][:, :], cbuf.ap),
                    [PS.key(bank)] + cbuf.keys, tbuf.keys)
                off = k * 128 - w * 512
                if -128 <= off <= 384:
                    mi = (off + 128) // 128
                    mk = dbm.ap[:, mi * 512:(mi + 1) * 512]
                    p.op("vector", lambda e, mk=mk: e.tensor_add(
                        tbuf.ap, tbuf.ap, mk), tbuf.keys + dbm.keys, tbuf.keys)
                p.op("scalar", lambda e, ex=exps[k]: e.activation(
                    ex.ap, tbuf.ap, AF.Exp), tbuf.keys, exps[k].keys)
            for k in range(16):
                def fso(e, k=k):
                    return e.matmul(ps[7][0:1, :], onesb.ap[:, 0:1],
                                    exps[k].ap, start=(k == 0), stop=(k == 15))
                p.op("tensor", fso, exps[k].keys + onesb.keys, [PS.key(7)])
            p.op("vector", lambda e: e.reciprocal(recs.ap[0:1, :],
                                                  ps[7][0:1, :]),
                 [PS.key(7)], recs.keys, "special")
            def frb(e):
                return e.matmul(ps[6][:, :], onesf.ap[0:1, :],
                                recs.ap[0:1, :], start=True, stop=True)
            p.op("tensor", frb, onesf.keys + recs.keys, [PS.key(6)])
            p.op("scalar", lambda e: e.activation(rb.ap, ps[6][:, :],
                                                  AF.Identity),
                 [PS.key(6)], rb.keys)
            for dc in range(8):
                bank = 4 + (dc % 2)
                for k in range(16):
                    lhsT = ftm.ap[:, k * 1024 + dc * 128:k * 1024 + (dc + 1) * 128]
                    def fav(e, lhsT=lhsT, k=k, bank=bank):
                        return e.matmul(ps[bank][:, :], lhsT, exps[k].ap,
                                        start=(k == 0), stop=(k == 15))
                    p.op("tensor", fav, ftm.keys + exps[k].keys,
                         [PS.key(bank)])
                p.op("vector", lambda e, b=bank: e.tensor_mul(
                    sfm.ap, ps[b][:, :], rb.ap),
                    [PS.key(bank)] + rb.keys, sfm.keys)
                p.dma("scalar",
                      p.dram["s1_fm"][dc * 128:(dc + 1) * 128, t0:t0 + 512],
                      sfm.ap, sfm.keys,
                      dk("s1_fm", dc * 128, (dc + 1) * 128, t0, t0 + 512))
    A.release()



def ph3_routing(p):
    A = p.A
    rta = A.alloc(32 * 4, F32)
    p.rta = rta
    A.mark()
    onesf = p.sb["onesf"]
    onesb = A.alloc(128, BF16)
    p.op("gpsimd", lambda e: e.memset(onesb.ap, 1.0), (), onesb.keys)
    idn = p.sb["idn_f32"]
    triu = p.sb["triu"]
    rw = A.alloc(NCH * 16, BF16)
    for j in range(NCH):
        p.dma("gpsimd", rw.ap[:, j * 16:(j + 1) * 16],
              p.dram["router_wT"][j * 128:(j + 1) * 128, :],
              dk("router_wT"), rw.keys)
    rbr = A.alloc(16, F32, parts=1)
    p.dma("sync", rbr.ap[0:1, :], p.dram["router_b"].rearrange("a 1 -> 1 a"),
          dk("router_b"), rbr.keys)
    rbt = A.alloc(16, F32)
    bcast_row(p, rbr.ap, rbr.keys, rbt, 16, bank=6)
    iob = A.alloc(16, F32, parts=1)
    p.dma("sync", iob.ap[0:1, :], p.dram["iota16"][:, :], dk("iota16"),
          iob.keys)
    iot = A.alloc(16, F32)
    bcast_row(p, iob.ap, iob.keys, iot, 16, bank=6)
    rhsw = A.alloc(512, BF16)
    lgt = A.alloc(512, F32, parts=16)
    ltm_w = A.alloc(64, F32)
    mx_w = A.alloc(32 * 2, F32)
    ix_w = A.alloc(32 * 2, mybir.dt.uint32)
    gwide = A.alloc(64, F32)
    H_w = A.alloc(32 * 16, BF16)
    snaps = A.alloc(33 * 16, F32, parts=1)
    Pt_all = A.alloc(32 * 16, F32)
    cnt_sb = A.alloc(16, F32, parts=8)
    rsel = A.alloc(1, F32, parts=8)
    slots_all = A.alloc(64, I32)
    tok_all = A.alloc(32, F32)
    badc = A.alloc(1, F32)
    iob1 = A.alloc(1, F32)
    Rrow = A.alloc(16, F32, parts=1)
    pw = A.alloc(32 * 16, F32)
    kw = A.alloc(32 * 8, F32)
    zg = A.alloc(8192, BF16)
    ps = p.ps
    table = p.dram["table_d"]

    p.dma("sync", iob1.ap[:, 0:1], p.dram["iota128"][:, :], dk("iota128"),
          iob1.keys)
    p.op("vector", lambda e: e.tensor_scalar_add(
        badc.ap[:, 0:1], iob1.ap[:, 0:1], float(NE * CAP)), iob1.keys,
        badc.keys, "special")
    zf = zg.ap[:, 0:2564].bitcast(F32)
    p.op("gpsimd", lambda e, zf=zf: e.memset(zf[:, 0:641], SENT), (), zg.keys)
    p.dma("gpsimd", table[0:82048, :].rearrange("(p f) 1 -> p f", p=128),
          zf[:, 0:641], zg.keys, dk("table_d"))
    p.op("gpsimd", lambda e: e.memset(zg.ap, 0.0), (), zg.keys)
    p.dma("gpsimd", p.dram["xall_d"][R * T:R * T + 1, :],
          zg.ap[0:1, 0:1024], zg.keys, dk("xall_d", R * T, R * T + 1))
    for i in range(10):
        p.dma("gpsimd" if i % 2 else "scalar",
              p.dram["packed_d"][i * 1024:(i + 1) * 1024, :].rearrange(
                  "(f p) c -> p f c", p=128),
              zg.ap[:, 0:8192].rearrange("p (f c) -> p f c", c=1024),
              zg.keys, dk("packed_d", i * 1024, (i + 1) * 1024))
    p.dma("gpsimd", p.dram["eall_d"][NE * CAP:NE * CAP + 1, :],
          zg.ap[0:1, 0:1024], zg.keys, dk("eall_d", NE * CAP, NE * CAP + 1))
    p.op("vector", lambda e: e.memset(snaps.ap[0:1, 0:16], 0.0), (),
         snaps.keys, "special")

    for wnd in range(8):
        t0 = wnd * 512
        for j in range(NCH):
            p.dma("sync", rhsw.ap[:, :],
                  p.dram["xn1_tm"][t0:t0 + 512, j * 128:(j + 1) * 128],
                  dk("xn1_tm", t0, t0 + 512, j * 128, (j + 1) * 128),
                  rhsw.keys, transpose=True)
            def fl(e, j=j):
                return e.matmul(ps[4][0:16, :], rw.ap[:, j * 16:(j + 1) * 16],
                                rhsw.ap, start=(j == 0), stop=(j == NCH - 1))
            p.op("tensor", fl, rw.keys + rhsw.keys, [PS.key(4)])
        p.op("scalar", lambda e: e.activation(lgt.ap[0:16, :], ps[4][0:16, :],
                                              AF.Identity),
             [PS.key(4)], lgt.keys)
        for sub in range(4):
            i = wnd * 4 + sub
            ltm = ltm_w.ap[:, (i % 2) * 16:(i % 2) * 16 + 16]
            mx8 = mx_w.ap[:, i * 2:(i + 1) * 2]
            ix8 = ix_w.ap[:, i * 2:(i + 1) * 2]
            mxs = gwide.ap[:, (i % 2) * 16 + 8:(i % 2) * 16 + 16]
            ixs = zg.ap[:, 4096 + (i % 4) * 32:4096 + (i % 4) * 32 + 16]\
                .bitcast(mybir.dt.uint32)
            def ftr(e, sub=sub, ltm=ltm):
                return e.transpose(ps[5][:, 0:16],
                                   lgt.ap[0:16, sub * 128:(sub + 1) * 128],
                                   idn.ap[0:16, 0:16])
            p.op("tensor", ftr, lgt.keys + idn.keys, [PS.key(5)])
            p.op("vector", lambda e, ltm=ltm: e.tensor_add(
                ltm, ps[5][:, 0:16], rbt.ap),
                [PS.key(5)] + rbt.keys, ltm_w.keys, "special")
            p.op("vector", lambda e, ltm=ltm, mxs=mxs: e.max(mxs, ltm),
                 ltm_w.keys, gwide.keys, "special")
            p.op("vector", lambda e, ltm=ltm, mxs=mxs, ixs=ixs: e.max_index(
                ixs, mxs, ltm), gwide.keys + ltm_w.keys, zg.keys, "special")
            e1 = p.rta.ap[:, i * 4:i * 4 + 1]
            e2 = p.rta.ap[:, i * 4 + 1:i * 4 + 2]
            g1 = p.rta.ap[:, i * 4 + 2:i * 4 + 3]
            g2 = p.rta.ap[:, i * 4 + 3:i * 4 + 4]
            t1 = gwide.ap[:, i % 2:i % 2 + 1]
            p.op("vector", lambda e, e1=e1, ixs=ixs: e.tensor_copy(
                e1, ixs[:, 0:1]), zg.keys, p.rta.keys, "special")
            p.op("vector", lambda e, e2=e2, ixs=ixs: e.tensor_copy(
                e2, ixs[:, 1:2]), zg.keys, p.rta.keys, "special")
            p.op("vector", lambda e, mx8=mx8, mxs=mxs: e.tensor_copy(
                mx8, mxs[:, 0:2]), gwide.keys, mx_w.keys, "special")
            p.op("vector", lambda e, ix8=ix8, ixs=ixs: e.tensor_copy(
                ix8, ixs[:, 0:2]), zg.keys, ix_w.keys, "special")
            p.op("vector", lambda e, t1=t1, mxs=mxs: e.tensor_sub(
                t1, mxs[:, 1:2], mxs[:, 0:1]), gwide.keys, gwide.keys,
                "special")
            p.op("scalar", lambda e, t1=t1: e.activation(t1, t1, AF.Exp),
                 gwide.keys, gwide.keys, "special")
            p.op("vector", lambda e, t1=t1: e.tensor_scalar_add(t1, t1, 1.0),
                 gwide.keys, gwide.keys, "special")
            p.op("vector", lambda e, g1=g1, t1=t1: e.reciprocal(g1, t1),
                 gwide.keys, p.rta.keys, "special")
            p.op("vector", lambda e, g1=g1, g2=g2: e.tensor_scalar(
                g2, g1, -1.0, 1.0, ALU.mult, ALU.add), p.rta.keys,
                p.rta.keys, "special")
            Hh = H_w.ap[:, i * 16:(i + 1) * 16]
            oh2v = zg.ap[:, 2048 + (i % 4) * 64:2048 + (i % 4) * 64 + 32]\
                .bitcast(BF16)
            p.op("vector", lambda e, Hh=Hh, e1=e1: e.tensor_scalar(
                Hh, iot.ap, e1, None, ALU.is_equal),
                iot.keys + p.rta.keys, H_w.keys, "special")
            p.op("vector", lambda e, oh2v=oh2v, e2=e2: e.tensor_scalar(
                oh2v[:, 0:16], iot.ap, e2, None, ALU.is_equal),
                iot.keys + p.rta.keys, zg.keys, "special")
            p.op("vector", lambda e, Hh=Hh, oh2v=oh2v: e.tensor_add(
                Hh, Hh, oh2v[:, 0:16]), zg.keys + H_w.keys, H_w.keys,
                "special")
            def fp1(e, Hh=Hh):
                return e.matmul(ps[6][:, 0:16], triu.ap, Hh,
                                start=True, stop=True)
            p.op("tensor", fp1, triu.keys + H_w.keys, [PS.key(6)])
            def fr2(e, i=i, Hh=Hh):
                return e.matmul(ps[7][0:1, 0:16], onesb.ap[:, 0:1], Hh,
                                start=(i == 0), stop=(i == 31))
            p.op("tensor", fr2, onesb.keys + H_w.keys, [PS.key(7)])
            if DBG and i == 0:
                p.dma("scalar", p.dram["dbg"][0:128, 0:16], ltm,
                      ltm_w.keys, dk("dbg", 0, 128, 0, 512))
                p.dma("scalar", p.dram["dbg"][0:128, 16:24], mxs,
                      gwide.keys, dk("dbg", 0, 128, 0, 512))
                p.dma("gpsimd", p.dram["dbg"][0:128, 24:32].bitcast(
                    mybir.dt.uint32), ixs, zg.keys, dk("dbg", 0, 128, 0, 512))
                p.dma("scalar", p.dram["dbg"][0:16, 32:544],
                      lgt.ap[0:16, :], lgt.keys, dk("dbg", 0, 128, 0, 1024))
            sdst = snaps.ap[0:1, (i + 1) * 16:(i + 2) * 16]
            p.op("scalar", lambda e, sdst=sdst: e.activation(
                sdst, ps[7][0:1, 0:16], AF.Identity),
                [PS.key(7)], snaps.keys, "special")
            dstp = Pt_all.ap[:, i * 16:(i + 1) * 16]
            p.op("scalar", lambda e, dstp=dstp: e.activation(
                dstp, ps[6][:, 0:16], AF.Identity),
                [PS.key(6)], Pt_all.keys)
    p.dma("gpsimd", p.dram["cnt_in"][0:1, :], snaps.ap[0:1, 32 * 16:33 * 16],
          snaps.keys, dk("cnt_in"))
    def fcoll(e):
        return e.collective_compute(
            "AllGather", ALU.bypass, replica_groups=[list(range(R))],
            ins=[p.dram["cnt_in"].ap().opt()],
            outs=[p.dram["cnt_all"].ap().opt()])
    p.bd.emit("gpsimd", fcoll, dk("cnt_in"), dk("cnt_all"), "coll")
    p.dma("sync", cnt_sb.ap[0:8, :], p.dram["cnt_all"][:, :], dk("cnt_all"),
          cnt_sb.keys)
    p.dma("sync", rsel.ap[0:8, :], p.dram["ranksel"][:, :], dk("ranksel"),
          rsel.keys)
    def foff(e):
        return e.matmul(ps[6][0:1, 0:16], rsel.ap[0:8, 0:1],
                        cnt_sb.ap[0:8, :], start=True, stop=True)
    p.op("tensor", foff, rsel.keys + cnt_sb.keys, [PS.key(6)])
    p.op("scalar", lambda e: e.activation(Rrow.ap[0:1, 0:16],
                                          ps[6][0:1, 0:16], AF.Identity),
         [PS.key(6)], Rrow.keys, "special")
    def fxag(e):
        return e.collective_compute(
            "AllGather", ALU.bypass, replica_groups=[list(range(R))],
            ins=[p.dram["xn1_tm"].ap().opt()],
            outs=[p.dram["xall_d"][0:R * T, :].opt()])
    p.bd.emit("gpsimd", fxag, dk("xn1_tm"), dk("xall_d", 0, R * T), "coll")
    orow_all = A.alloc(512, F32)
    for i in range(32):
        obr = snaps.ap[0:1, i * 16:(i + 1) * 16]
        dst = orow_all.ap[0:1, i * 16:(i + 1) * 16]
        p.op("vector", lambda e, obr=obr, dst=dst: e.tensor_add(
            dst, obr, Rrow.ap[0:1, 0:16]),
            snaps.keys + Rrow.keys, orow_all.keys, "special")
    offs_all = A.alloc(512, F32)
    def fob(e):
        return e.matmul(ps[5][:, :], onesf.ap[0:1, :],
                        orow_all.ap[0:1, :], start=True, stop=True)
    p.op("tensor", fob, orow_all.keys + onesf.keys, [PS.key(5)])
    p.op("scalar", lambda e: e.activation(
        offs_all.ap, ps[5][:, :], AF.Identity), [PS.key(5)], offs_all.keys,
        "special")
    for i in range(32):
        Pv = Pt_all.ap[:, i * 16:(i + 1) * 16]
        e1 = p.rta.ap[:, i * 4:i * 4 + 1]
        e2 = p.rta.ap[:, i * 4 + 1:i * 4 + 2]
        g1 = p.rta.ap[:, i * 4 + 2:i * 4 + 3]
        g2 = p.rta.ap[:, i * 4 + 3:i * 4 + 4]
        kc = kw.ap[:, i * 8:(i + 1) * 8]
        pos1, pos2, k1, k2, d1, den, rr, tm2 = (
            kc[:, 0:1], kc[:, 1:2], kc[:, 2:3], kc[:, 3:4], kc[:, 4:5],
            kc[:, 5:6], kc[:, 6:7], kc[:, 7:8])
        oft = offs_all.ap[:, i * 16:(i + 1) * 16]
        Pp = pw.ap[:, i * 16:(i + 1) * 16]
        p.op("vector", lambda e, Pv=Pv, Pp=Pp, oft=oft: e.tensor_add(
            Pp, Pv, oft), Pt_all.keys + offs_all.keys, pw.keys, "special")
        oh1 = zg.ap[:, 4096 + (i % 4) * 128:4096 + (i % 4) * 128 + 64]\
            .bitcast(F32)
        p.op("vector", lambda e, oh1=oh1, e1=e1: e.tensor_scalar(
            oh1[:, 0:16], iot.ap, e1, None, ALU.is_equal),
            iot.keys + p.rta.keys, zg.keys, "special")
        p.op("vector", lambda e, oh1=oh1, e2=e2: e.tensor_scalar(
            oh1[:, 16:32], iot.ap, e2, None, ALU.is_equal),
            iot.keys + p.rta.keys, zg.keys, "special")
        p.op("vector", lambda e, oh1=oh1, Pp=Pp: e.tensor_mul(
            oh1[:, 0:16], oh1[:, 0:16], Pp), zg.keys + pw.keys, zg.keys,
            "special")
        p.op("vector", lambda e, oh1=oh1, Pp=Pp: e.tensor_mul(
            oh1[:, 16:32], oh1[:, 16:32], Pp), zg.keys + pw.keys, zg.keys,
            "special")
        p.op("vector", lambda e, oh1=oh1, pos1=pos1: e.reduce_sum(
            pos1, oh1[:, 0:16], axis=AX.X), zg.keys, kw.keys, "special")
        p.op("vector", lambda e, oh1=oh1, pos2=pos2: e.reduce_sum(
            pos2, oh1[:, 16:32], axis=AX.X), zg.keys, kw.keys, "special")
        p.op("vector", lambda e, pos1=pos1, k1=k1: e.tensor_single_scalar(
            k1, pos1, float(CAP), ALU.is_lt), kw.keys, kw.keys, "special")
        p.op("vector", lambda e, pos2=pos2, k2=k2: e.tensor_single_scalar(
            k2, pos2, float(CAP), ALU.is_lt), kw.keys, kw.keys, "special")
        p.op("vector", lambda e, g1=g1, k1=k1, d1=d1: e.tensor_mul(
            d1, g1, k1), p.rta.keys + kw.keys, kw.keys, "special")
        p.op("vector", lambda e, g2=g2, k2=k2, d1=d1, den=den:
             e.scalar_tensor_tensor(den, g2, k2, d1, ALU.mult, ALU.add),
             p.rta.keys + kw.keys, kw.keys, "special")
        p.op("vector", lambda e, den=den: e.tensor_scalar_add(
            den, den, 1e-9), kw.keys, kw.keys, "special")
        p.op("vector", lambda e, den=den, rr=rr: e.reciprocal(rr, den),
             kw.keys, kw.keys, "special")
        p.op("vector", lambda e, d1=d1, rr=rr, g1=g1: e.tensor_mul(
            g1, d1, rr), kw.keys, p.rta.keys, "special")
        p.op("vector", lambda e, g2=g2, k2=k2, tm2=tm2: e.tensor_mul(
            tm2, g2, k2), p.rta.keys + kw.keys, kw.keys, "special")
        p.op("vector", lambda e, tm2=tm2, rr=rr, g2=g2: e.tensor_mul(
            g2, tm2, rr), kw.keys, p.rta.keys, "special")
        for kk, (ek, posk, kcol) in enumerate(((e1, pos1, k1),
                                               (e2, pos2, k2))):
            sv = zg.ap[:, 6144 + (i % 2) * 128 + kk * 64:
                       6144 + (i % 2) * 128 + kk * 64 + 32].bitcast(F32)
            svi = zg.ap[:, 6144 + (i % 2) * 128 + kk * 64:
                        6144 + (i % 2) * 128 + kk * 64 + 32].bitcast(I32)
            p.op("vector", lambda e, sv=sv, ek=ek, posk=posk:
                 e.scalar_tensor_tensor(sv[:, 2:3], ek, float(CAP), posk,
                                        ALU.mult, ALU.add),
                 p.rta.keys + kw.keys, zg.keys, "special")
            p.op("vector", lambda e, sv=sv: e.tensor_sub(
                sv[:, 0:1], sv[:, 2:3], badc.ap[:, 0:1]),
                zg.keys + badc.keys, zg.keys, "special")
            p.op("vector", lambda e, sv=sv, kcol=kcol: e.tensor_mul(
                sv[:, 1:2], sv[:, 0:1], kcol), zg.keys + kw.keys, zg.keys,
                "special")
            p.op("vector", lambda e, sv=sv: e.tensor_add(
                sv[:, 0:1], sv[:, 1:2], badc.ap[:, 0:1]),
                zg.keys + badc.keys, zg.keys, "special")
            iv = slots_all.ap[:, i * 2 + kk:i * 2 + kk + 1]
            p.op("vector", lambda e, sv=sv, iv=iv: e.tensor_copy(
                iv, sv[:, 0:1]), zg.keys, slots_all.keys, "special")
            p.op("vector", lambda e, sv=sv, ek=ek: e.tensor_scalar(
                sv[:, 3:4], ek, 0.5, -0.25, ALU.mult, ALU.add),
                p.rta.keys, zg.keys, "special")
            p.op("vector", lambda e, sv=sv, svi=svi: e.tensor_copy(
                svi[:, 4:5], sv[:, 3:4]), zg.keys, zg.keys, "special")
            p.op("vector", lambda e, sv=sv, svi=svi: e.tensor_copy(
                sv[:, 5:6], svi[:, 4:5]), zg.keys, zg.keys, "special")
            p.op("vector", lambda e, sv=sv, ek=ek, posk=posk:
                 e.scalar_tensor_tensor(sv[:, 6:7], ek, 8.0 * CAP, posk,
                                        ALU.mult, ALU.add),
                 p.rta.keys + kw.keys, zg.keys, "special")
            p.op("vector", lambda e, sv=sv, ek=ek:
                 e.scalar_tensor_tensor(ek, sv[:, 5:6], -15.0 * CAP,
                                        sv[:, 6:7], ALU.mult, ALU.add),
                 zg.keys, p.rta.keys, "special")
    for i in range(32):
        p.dma("sync", tok_all.ap[:, i:i + 1],
              p.dram["tokid"][i * 128:(i + 1) * 128, :], dk("tokid"),
              tok_all.keys)
    for i in range(32):
        for kk in range(2):
            iv = slots_all.ap[:, i * 2 + kk:i * 2 + kk + 1]
            tv = tok_all.ap[:, i:i + 1]
            def fsc(e, iv=iv, tv=tv):
                return e.indirect_dma_start(
                    out=table[:, :], out_offset=IndirectOffsetOnAxis(iv, 0),
                    in_=tv, in_offset=None)
            p.bd.emit("gpsimd", fsc, slots_all.keys + tok_all.keys,
                      dk("table_d"), "dma")
    def ftmin(e):
        return e.collective_compute(
            "AllReduce", ALU.min, replica_groups=[list(range(R))],
            ins=[table.ap().opt()], outs=[p.dram["table_min"].ap().opt()])
    p.bd.emit("gpsimd", ftmin, dk("table_d"), dk("table_min"), "coll")
    A.release()


def ph3_gather_packed(p):
    A = p.A
    A.mark()
    NG = 2 * CAP // 128
    ms_all = A.alloc(NG, I32)
    tg_all = A.alloc(NG, F32)
    tg_i = A.alloc(NG, I32)
    gbufs = [A.alloc(1024, BF16) for _ in range(4)]
    tmin = p.dram["table_min"]
    for g in range(NG):
        p.dma("sync", ms_all.ap[:, g:g + 1],
              p.dram["myslot_i"][g * 128:(g + 1) * 128, :],
              dk("myslot_i"), ms_all.keys)
    for g in range(NG):
        def fg1(e, g=g):
            return e.indirect_dma_start(
                out=tg_all.ap[:, g:g + 1], out_offset=None,
                in_=tmin[:, :], in_offset=IndirectOffsetOnAxis(
                    ms_all.ap[:, g:g + 1], 0))
        p.bd.emit("gpsimd", fg1, ms_all.keys + dk("table_min"), tg_all.keys,
                  "dma")
    p.op("vector", lambda e: e.tensor_scalar_min(
        tg_all.ap, tg_all.ap, float(R * T)), tg_all.keys, tg_all.keys)
    p.op("vector", lambda e: e.tensor_copy(tg_i.ap, tg_all.ap),
         tg_all.keys, tg_i.keys)
    p.op("gpsimd", lambda e: e.tensor_copy(gbufs[0].ap[:, 0:8],
                                           gbufs[0].ap[:, 0:8]),
         gbufs[0].keys, gbufs[0].keys)
    for g in range(NG):
        gb = gbufs[g % 4]
        def fg2(e, g=g, gb=gb):
            return e.indirect_dma_start(
                out=gb.ap, out_offset=None,
                in_=p.dram["xall_d"][:, :],
                in_offset=IndirectOffsetOnAxis(tg_i.ap[:, g:g + 1], 0))
        p.bd.emit("gpsimd", fg2, tg_i.keys + dk("xall_d"), gb.keys, "dma")
        p.dma("gpsimd", p.dram["packed_d"][g * 128:(g + 1) * 128, :],
              gb.ap, gb.keys, dk("packed_d", g * 128, (g + 1) * 128))
    A.release()



def ph5_experts(p):
    A = p.A
    A.mark()
    w1sb = A.alloc(NCH * DFF, BF16)
    w2s = [A.alloc(32 * 128, BF16) for _ in range(2)]
    b1c = A.alloc(32, F32)
    b2c = A.alloc(8, F32)
    rhs_all = A.alloc(NCH * 512, BF16)
    rhsw = [View(rhs_all.ap[:, j * 512:(j + 1) * 512], rhs_all.keys, 0, 0)
            for j in range(NCH)]
    hsb = A.alloc(32 * 512, BF16)
    eo = A.alloc(512, BF16)
    eot = A.alloc(1024, BF16)
    ps = p.ps
    p.dma("gpsimd", p.dram["w2bf_d"][:, :], p.dram["moe_w2"][:, :],
          dk("moe_w2"), dk("w2bf_d"))
    for ex in range(EPC):
        for j in range(NCH):
            p.dma("gpsimd", w1sb.ap[:, j * DFF:(j + 1) * DFF],
                  p.dram["moe_w1"][ex * D + j * 128:ex * D + (j + 1) * 128, :],
                  dk("moe_w1"), w1sb.keys)
        for i in range(32):
            p.dma("sync", b1c.ap[:, i:i + 1],
                  p.dram["moe_b1"][ex * DFF + i * 128:ex * DFF + (i + 1) * 128, :],
                  dk("moe_b1"), b1c.keys)
        for i in range(8):
            p.dma("sync", b2c.ap[:, i:i + 1],
                  p.dram["moe_b2"][ex * D + i * 128:ex * D + (i + 1) * 128, :],
                  dk("moe_b2"), b2c.keys)
        for g in range(CAP // 512):
            s0 = ex * CAP + g * 512
            for j in range(NCH):
                p.dma("sync", rhsw[j].ap,
                      p.dram["packed_d"][s0:s0 + 512, j * 128:(j + 1) * 128],
                      dk("packed_d", s0, s0 + 512), rhsw[j].keys,
                      transpose=True)
            for fc in range(32):
                bank = fc % 2
                for j in range(NCH):
                    lhsT = w1sb.ap[:, j * DFF + fc * 128:j * DFF + (fc + 1) * 128]
                    def fm1(e, lhsT=lhsT, rv=rhsw[j], bank=bank, j=j):
                        return e.matmul(ps[bank][:, :], lhsT, rv.ap,
                                        start=(j == 0), stop=(j == NCH - 1))
                    p.op("tensor", fm1, w1sb.keys + rhsw[j].keys,
                         [PS.key(bank)])
                hdst = hsb.ap[:, fc * 512:(fc + 1) * 512]
                p.op("scalar", lambda e, b=bank, fc=fc, hdst=hdst:
                     e.activation(hdst, ps[b][:, :], AF.Gelu,
                                  bias=b1c.ap[:, fc:fc + 1]),
                     [PS.key(bank)] + b1c.keys, hsb.keys)
            for oc in range(8):
                bank = 2 + (oc % 2)
                lv = w2s[oc % 2]
                ap_in = p.dram["w2bf_d"][ex * DFF:(ex + 1) * DFF, :].rearrange(
                    "(j p) c -> p j c", p=128)[:, :, oc * 128:(oc + 1) * 128]
                def flw(e, lv=lv, ap_in=ap_in):
                    return e.dma_start(
                        out=lv.ap.rearrange("p (j c) -> p j c", c=128),
                        in_=ap_in)
                p.bd.emit("gpsimd", flw, dk("w2bf_d"), lv.keys, "dma")
                for j in range(32):
                    lhsT = lv.ap[:, j * 128:(j + 1) * 128]
                    rhs = hsb.ap[:, j * 512:(j + 1) * 512]
                    def fm2(e, lhsT=lhsT, rhs=rhs, bank=bank, j=j, lv=lv):
                        return e.matmul(ps[bank][:, :], lhsT, rhs,
                                        start=(j == 0), stop=(j == 31))
                    p.op("tensor", fm2, lv.keys + hsb.keys, [PS.key(bank)])
                p.op("scalar", lambda e, b=bank, oc=oc: e.activation(
                    eo.ap, ps[b][:, :], AF.Identity,
                    bias=b2c.ap[:, oc:oc + 1]),
                    [PS.key(bank)] + b2c.keys, eo.keys)
                p.dma("scalar",
                      p.dram["eo_fm_d"][oc * 128:(oc + 1) * 128,
                                        s0:s0 + 512],
                      eo.ap, eo.keys,
                      dk("eo_fm_d", oc * 128, (oc + 1) * 128, s0, s0 + 512))
        for t in range(ex * CAP // 128, (ex + 1) * CAP // 128):
            r0 = t * 128
            p.dma("sync", eot.ap, p.dram["eo_fm_d"][:, r0:r0 + 128],
                  dk("eo_fm_d", 0, D, r0, r0 + 128), eot.keys, transpose=True)
            p.dma("scalar", p.dram["eout_d"][r0:r0 + 128, :], eot.ap,
                  eot.keys, dk("eout_d", r0, r0 + 128))
        def feag(e, ex=ex):
            return e.collective_compute(
                "AllGather", ALU.bypass, replica_groups=[list(range(R))],
                ins=[p.dram["eout_d"][ex * CAP:(ex + 1) * CAP, :].opt()],
                outs=[p.dram["eall_d"][ex * R * CAP:(ex + 1) * R * CAP,
                                       :].opt()])
        p.bd.emit("gpsimd", feag, dk("eout_d", ex * CAP, (ex + 1) * CAP),
                  dk("eall_d", ex * R * CAP, (ex + 1) * R * CAP), "coll")
    A.release()



def ph6_final(p):
    A = p.A
    A.mark()
    NB = 2
    x1s = [A.alloc(1024, F32) for _ in range(NB)]
    s1s = [A.alloc(1024, BF16) for _ in range(NB)]
    r1s = [A.alloc(1024, BF16) for _ in range(NB)]
    r2s = [A.alloc(1024, BF16) for _ in range(NB)]
    mos = [A.alloc(1024, F32) for _ in range(NB)]
    gbs = [A.alloc(1024, BF16) for _ in range(NB)]
    ivs = [A.alloc(2, I32) for _ in range(NB)]
    scr = [A.alloc(4, F32) for _ in range(NB)]
    for i in range(32):
        r0 = i * 128
        x1, s1, r1, r2, mo, gb, iv, sc = (
            x1s[i % NB], s1s[i % NB], r1s[i % NB], r2s[i % NB], mos[i % NB],
            gbs[i % NB], ivs[i % NB], scr[i % NB])
        p.dma("sync", x1.ap, p.dram["x1_tm"][r0:r0 + 128, :],
              dk("x1_tm", r0, r0 + 128), x1.keys)
        p.dma("sync", s1.ap, p.dram["s1_fm"][:, r0:r0 + 128],
              dk("s1_fm", 0, D, r0, r0 + 128), s1.keys, transpose=True)
        slot1 = p.rta.ap[:, i * 4:i * 4 + 1]
        slot2 = p.rta.ap[:, i * 4 + 1:i * 4 + 2]
        g1 = p.rta.ap[:, i * 4 + 2:i * 4 + 3]
        g2 = p.rta.ap[:, i * 4 + 3:i * 4 + 4]
        for kk, sl in ((0, slot1), (1, slot2)):
            p.op("vector", lambda e, sl=sl, sc=sc, kk=kk: e.tensor_scalar_min(
                sc.ap[:, kk:kk + 1], sl, float(NE * CAP)),
                p.rta.keys, sc.keys, "special")
            p.op("vector", lambda e, sc=sc, iv=iv, kk=kk: e.tensor_copy(
                iv.ap[:, kk:kk + 1], sc.ap[:, kk:kk + 1]), sc.keys, iv.keys,
                "special")
        def fga(e, iv=iv, r1=r1):
            return e.indirect_dma_start(
                out=r1.ap, out_offset=None, in_=p.dram["eall_d"][:, :],
                in_offset=IndirectOffsetOnAxis(iv.ap[:, 0:1], 0))
        p.bd.emit("gpsimd", fga, iv.keys + dk("eall_d"), r1.keys, "dma")
        def fgb(e, iv=iv, r2=r2):
            return e.indirect_dma_start(
                out=r2.ap, out_offset=None, in_=p.dram["eall_d"][:, :],
                in_offset=IndirectOffsetOnAxis(iv.ap[:, 1:2], 0))
        p.bd.emit("gpsimd", fgb, iv.keys + dk("eall_d"), r2.keys, "dma")
        p.op("vector", lambda e, r1=r1, g1=g1, mo=mo: e.tensor_scalar(
            mo.ap, r1.ap, g1, None, ALU.mult), r1.keys + p.rta.keys, mo.keys)
        p.op("vector", lambda e, r2=r2, g2=g2, mo=mo: e.scalar_tensor_tensor(
            mo.ap, r2.ap, g2, mo.ap, ALU.mult, ALU.add),
            r2.keys + p.rta.keys + mo.keys, mo.keys)
        p.op("scalar", lambda e, mo=mo, gb=gb: e.activation(
            gb.ap, mo.ap, AF.Silu), mo.keys, gb.keys)
        p.op("vector", lambda e, s1=s1, gb=gb, mo=mo: e.tensor_mul(
            mo.ap, s1.ap, gb.ap), s1.keys + gb.keys, mo.keys)
        p.op("vector", lambda e, x1=x1, mo=mo: e.tensor_add(
            x1.ap, x1.ap, mo.ap), x1.keys + mo.keys, x1.keys)
        p.dma("scalar", p.dram["out_d"][r0:r0 + 128, :], x1.ap, x1.keys,
              dk("out_d", r0, r0 + 128))
    A.release()



def declare_dram(nc, names, external=()):
    d = {}
    ext = dict(external)
    for name, shape, dt in names:
        kind = ext.get(name, "Internal")
        d[name] = nc.dram_tensor(name, shape, dt, kind=kind)
    return d


DRAM_SPECS = {
    "x0": ([T, D], F32),
    "xn0_tm": ([T, D], BF16),
    "q_fm": ([D, T], BF16),
    "v_fm": ([D, T], BF16),
    "s0_fm": ([D, T], BF16),
    "w_qkv": ([D, 3 * D], F32),
    "dbg": ([1280, 4096], F32),
    "w_proj": ([D, D], F32),
    "b_proj": ([D, 1], F32),
    "w1T": ([D, DFF], F32),
    "b1": ([DFF, 1], F32),
    "ln1_w": ([D, 1], F32),
    "ln1_b": ([D, 1], F32),
    "ln2_w": ([DFF, 1], F32),
    "ln2_b": ([DFF, 1], F32),
    "w2T": ([DFF, D], F32),
    "b2": ([D, 1], F32),
    "wq1_d": ([DFF // 128, NCH, 128, 128], BF16),
    "wq2_d": ([NCH, DFF // 128, 128, 128], BF16),
    "qx_d": ([T, D], BF16),
    "qx2_d": ([T, DFF], BF16),
    "h_fm": ([DFF, T], BF16),
    "g0_fm": ([D, T], BF16),
    "colf1_d": ([1, T], F32),
    "colf2_d": ([1, T], F32),
    "x1_tm": ([T, D], F32),
    "xn1_tm": ([T, D], BF16),
    "l1W": ([D, D], F32),
    "wqkv_bf": ([D, 3 * D], BF16),
    "wproj_bf": ([D, D], BF16),
    "l1b": ([D, 1], F32),
    "f_fm": ([D, T], BF16),
    "s1_fm": ([D, T], BF16),
    "router_wT": ([D, NE], F32),
    "router_b": ([NE, 1], F32),
    "tokid": ([T, 1], F32),
    "ranksel": ([R, 1], F32),
    "myslot_base": ([EPC * CAP, 1], F32),
    "myslot_i": ([EPC * CAP, 1], I32),
    "iota128": ([128, 1], F32),
    "cnt_in": ([1, NE], F32),
    "cnt_all": ([R, NE], F32),
    "table_d": ([NE * CAP + 128, 1], F32),
    "table_min": ([NE * CAP + 128, 1], F32),
    "xall_d": ([R * T + 1, D], BF16),
    "packed_d": ([EPC * CAP, D], BF16),
    "moe_w1": ([EPC * D, DFF], F32),
    "moe_b1": ([EPC * DFF, 1], F32),
    "moe_w2": ([EPC * DFF, D], F32),
    "moe_b2": ([EPC * D, 1], F32),
    "w2bf_d": ([EPC * DFF, D], BF16),
    "eo_fm_d": ([D, EPC * CAP], BF16),
    "eout_d": ([EPC * CAP, D], BF16),
    "eall_d": ([NE * CAP + 1, D], BF16),
    "out_d": ([T, D], F32),
    "table_dbg": ([NE * CAP + 128, 1], F32),
    "cnt_dbg": ([R, NE], F32),
}

ALL_PHASES = None


def ph_rta_dump(p):
    p.dma("scalar", p.dram["dbg"][700:828, 0:128], p.rta.ap, p.rta.keys,
          dk("dbg", 512, 1024, 0, 512))


def ph_dbgcopy(p):
    p.dma("gpsimd", p.dram["table_dbg"][:, :], p.dram["table_min"][:, :],
          dk("table_min"), dk("table_dbg"))
    p.dma("gpsimd", p.dram["cnt_dbg"][:, :], p.dram["cnt_all"][:, :],
          dk("cnt_all"), dk("cnt_dbg"))


INPUT_NAMES = [
    "x0", "w_qkv", "w_proj", "b_proj", "w1T", "b1", "ln1_w", "ln1_b",
    "ln2_w", "ln2_b", "w2T", "b2", "l1W", "l1b", "router_wT", "router_b",
    "moe_w1", "moe_b1", "moe_w2", "moe_b2", "tokid", "ranksel",
    "myslot_base", "myslot_i", "iota128",
]
ALL_CONSTS = ("rope_cos", "rope_sin", "rope_perm", "causal_masks",
              "idn_f32", "triu", "distbias_m", "iota16")


def full_phases():
    return [ph0_ln0, ph2_wquant, ph1_attn, ph2_all, ph3_routing,
            ph4_attn1, ph3_gather_packed, ph5_experts, ph6_final]


def make_in_maps(inputs):
    c = host_constants()
    x = np.ascontiguousarray(np.asarray(inputs["L0_attn_x"],
                                        np.float32)).reshape(B * S, D)
    w1 = np.asarray(inputs["L0_ffn_w1"], np.float32)
    w2 = np.asarray(inputs["L0_ffn_w2"], np.float32)
    rw = np.asarray(inputs["L1_ffn_router_w"], np.float32)
    base = {
        "w_qkv": np.asarray(inputs["L0_attn_w_qkv"], np.float32),
        "w_proj": np.asarray(inputs["L0_attn_w_proj"], np.float32),
        "b_proj": np.asarray(inputs["L0_attn_b_proj"],
                             np.float32).reshape(D, 1),
        "w1T": np.ascontiguousarray(w1.T),
        "b1": np.asarray(inputs["L0_ffn_b1"], np.float32).reshape(DFF, 1),
        "ln1_w": np.asarray(inputs["L0_ffn_ln1_w"], np.float32).reshape(D, 1),
        "ln1_b": np.asarray(inputs["L0_ffn_ln1_b"], np.float32).reshape(D, 1),
        "ln2_w": np.asarray(inputs["L0_ffn_ln2_w"],
                            np.float32).reshape(DFF, 1),
        "ln2_b": np.asarray(inputs["L0_ffn_ln2_b"],
                            np.float32).reshape(DFF, 1),
        "w2T": np.ascontiguousarray(w2.T),
        "b2": np.asarray(inputs["L0_ffn_b2"], np.float32).reshape(D, 1),
        "l1W": np.asarray(inputs["L1_attn_W"], np.float32),
        "l1b": np.asarray(inputs["L1_attn_b"], np.float32).reshape(D, 1),
        "router_wT": np.ascontiguousarray(rw.T),
        "router_b": np.asarray(inputs["L1_ffn_router_b"],
                               np.float32).reshape(NE, 1),
    }
    for n in CONST_SPECS:
        base[n] = np.asarray(c[n])
    mw1 = np.asarray(inputs["L1_ffn_w1"], np.float32)
    mb1 = np.asarray(inputs["L1_ffn_b1"], np.float32)
    mw2 = np.asarray(inputs["L1_ffn_w2"], np.float32)
    mb2 = np.asarray(inputs["L1_ffn_b2"], np.float32)
    in_maps = []
    for core in range(R):
        m = dict(base)
        m["x0"] = np.ascontiguousarray(x[core * T:(core + 1) * T])
        m["moe_w1"] = np.ascontiguousarray(
            mw1[core * EPC:(core + 1) * EPC].reshape(EPC * D, DFF))
        m["moe_b1"] = np.ascontiguousarray(
            mb1[core * EPC:(core + 1) * EPC].reshape(EPC * DFF, 1))
        m["moe_w2"] = np.ascontiguousarray(
            mw2[core * EPC:(core + 1) * EPC].reshape(EPC * DFF, D))
        m["moe_b2"] = np.ascontiguousarray(
            mb2[core * EPC:(core + 1) * EPC].reshape(EPC * D, 1))
        m["tokid"] = (core * T + np.arange(T, dtype=np.float32)
                      ).reshape(T, 1)
        m["ranksel"] = (np.arange(R) < core).astype(np.float32).reshape(R, 1)
        m["myslot_base"] = (core * EPC * CAP + np.arange(
            EPC * CAP, dtype=np.float32)).reshape(EPC * CAP, 1)
        m["myslot_i"] = (core * EPC * CAP + np.arange(
            EPC * CAP, dtype=np.int32)).reshape(EPC * CAP, 1).astype(np.int32)
        m["iota128"] = np.arange(128, dtype=np.float32).reshape(128, 1)
        in_maps.append(m)
    return in_maps


_NC_CACHE = {}


def kernel(**inputs):
    if "nc" not in _NC_CACHE:
        ext = {n: "ExternalInput" for n in INPUT_NAMES}
        ext["out_d"] = "ExternalOutput"
        _NC_CACHE["nc"] = build_test(full_phases(), external=ext,
                                     consts=ALL_CONSTS)
    nc = _NC_CACHE["nc"]
    in_maps = make_in_maps(inputs)
    res = run_bass_kernel_spmd(nc, in_maps, core_ids=list(range(R)))
    out = np.concatenate([r["out_d"] for r in res.results], axis=0)
    return out.reshape(B, S, D)


def _mk_nc():
    return bass.Bass("TRN2", target_bir_lowering=False)


PHASES = {}


def build_test(phases, external, consts=(), arena_kb=204, terminal_keys=()):
    nc = _mk_nc()
    names = [(n, *DRAM_SPECS[n]) for n in DRAM_SPECS]
    names += [(n, *CONST_SPECS[n]) for n in CONST_SPECS]
    ext = dict(external)
    for n in CONST_SPECS:
        ext[n] = "ExternalInput"
    dram = declare_dram(nc, names, external=ext)
    with contextlib.ExitStack() as ctx:
        sems = {}
        for e in ENGINES:
            sems["eng_" + e] = ctx.enter_context(nc.semaphore("se_" + e))
        for e in DMA_QUEUES:
            for ln in range(NLANE):
                sems[f"dma_{e}{ln}"] = ctx.enter_context(
                    nc.semaphore(f"sd_{e}{ln}"))
        sems["coll"] = ctx.enter_context(nc.semaphore("s_coll"))
        A = Arena(nc, ctx, "Amain", arena_kb * 1024)
        ps = PS(nc, ctx)
        bd = Build()
        p = P(nc, bd, dram, A, ps)
        p.ded_mx8 = ctx.enter_context(nc.sbuf_tensor("dmx8", [128, 8], F32))
        p.ded_ix8 = ctx.enter_context(
            nc.sbuf_tensor("dix8", [128, 8], mybir.dt.uint32))
        p.ded_cols = ctx.enter_context(nc.sbuf_tensor("dcols", [128, 32], F32))
        p.load_consts(list(consts))
        for ph in phases:
            ph(p)
            if DBG_ARENA:
                print(f"[arena] after {ph.__name__}: cur={A.cur} bytes/part")
        tk = list(terminal_keys)
        for n, kind in external.items():
            if kind == "ExternalOutput":
                tk += dk(n) if DRAM_SPECS.get(n, CONST_SPECS.get(n)) else [n]
        p.bd.emit("sync", lambda e: e.nop(), _flat([tk]), (), "compute", "end")
        globals()["_last_p"] = p
        bd.emit_program(nc, sems)
    return nc



